# version 26
# baseline (speedup 1.0000x reference)
import math
from contextlib import ExitStack

import numpy as np
import concourse.bass as bass
import concourse.mybir as mybir
from concourse.bass_utils import run_bass_kernel_spmd

F32 = mybir.dt.float32
BF16 = mybir.dt.bfloat16
AF = mybir.ActivationFunctionType
ALU = mybir.AluOpType

P = 128
D = 2048
C = 16
S = 2048
SO = 1024
H = 16
HD = 128
SB = 512
NSB = SO // SB
NT = S // P
EPS = 1e-5
SCALE = 1.0 / math.sqrt(HD)
NITER = H * NSB


class Ctr:
    def __init__(self, sem):
        self.sem = sem
        self.n = 0

    def inc(self, instr, k=1):
        instr.then_inc(self.sem, k)
        self.n += k
        return self.n


def build_nc(debug=False):
    nc = bass.Bass("TRN2", target_bir_lowering=False)

    xT = nc.dram_tensor("xT", [D, S], F32, kind="ExternalInput").ap()
    wq = nc.dram_tensor("wq", [D, D], F32, kind="ExternalInput").ap()
    wk = nc.dram_tensor("wk", [D, HD], F32, kind="ExternalInput").ap()
    wv = nc.dram_tensor("wv", [D, HD], F32, kind="ExternalInput").ap()
    wo = nc.dram_tensor("wo", [D, D], F32, kind="ExternalInput").ap()
    wt = nc.dram_tensor("wt", [D, D], F32, kind="ExternalInput").ap()
    sw = nc.dram_tensor("sw", [1, D], F32, kind="ExternalInput").ap()
    bias = nc.dram_tensor("bias", [1, D], F32, kind="ExternalInput").ap()
    y = nc.dram_tensor("y", [D, SO], F32, kind="ExternalOutput").ap()

    dbg = {}
    if debug:
        dbg["k"] = nc.dram_tensor("dbg_k", [P, S], BF16, kind="ExternalOutput").ap()
        dbg["v"] = nc.dram_tensor("dbg_v", [P, NT, HD], BF16, kind="ExternalOutput").ap()
        dbg["q"] = nc.dram_tensor("dbg_q", [P, H, SO], BF16, kind="ExternalOutput").ap()
        dbg["m"] = nc.dram_tensor("dbg_m", [1, S], F32, kind="ExternalOutput").ap()
        dbg["r"] = nc.dram_tensor("dbg_r", [1, S], F32, kind="ExternalOutput").ap()
        dbg["o"] = nc.dram_tensor("dbg_o", [P, H, SO], BF16, kind="ExternalOutput").ap()
        dbg["x1"] = nc.dram_tensor("dbg_x1", [P, C, SO], F32, kind="ExternalOutput").ap()

    xT_r = xT.rearrange("(c p) s -> p c s", p=P)
    wq_r = wq.rearrange("(c p) f -> p c f", p=P)
    wk_r = wk.rearrange("(c p) h -> p c h", p=P)
    wv_r = wv.rearrange("(c p) h -> p c h", p=P)
    wo_r = wo.rearrange("(c p) f -> p c f", p=P)
    wt_r = wt.rearrange("(c p) f -> p c f", p=P)
    y_r = y.rearrange("(c p) s -> p c s", p=P)

    WA = ["wq0", "wq2", "wo0", "wo2", "wo4", "wo6",
          "wob0", "wob2", "wob4", "wob6", "wt0", "wt2", "wt4", "wt6"]
    WB = ["wq1", "wo1", "wo3", "wo5", "wo7",
          "wob1", "wob3", "wob5", "wob7", "wt1", "wt3", "wt5", "wt7"]

    marks = {"stat": [None] * C, "v": [None] * 4, "q": [None] * C,
             "qk": [None] * (NITER * 8), "pv": [None] * NITER,
             "dones": [None] * NITER, "dbc": [None] * NITER,
             "wo": [None] * 32, "stat2": [None] * C, "ffn": [None] * 32,
             "q2": [None] * 8}

    with ExitStack() as es:
        def sbuf(*a):
            return es.enter_context(nc.sbuf_tensor(*a))
        cst = sbuf("cst", [P, 6], F32)
        ones_c = sbuf("ones_c", [P, 1], BF16)
        ones_r = sbuf("ones_r", [1, P], F32)
        ones_rb = sbuf("ones_rb", [1, P], BF16)
        drow_hi = sbuf("drow_hi", [1, SB], BF16)
        drow_lo = sbuf("drow_lo", [1, SB], BF16)
        xbuf = sbuf("xbuf", [P, C, S], BF16)
        Q_sb = sbuf("q_sb", [P, H, SO], BF16)
        K_sb = sbuf("k_sb", [P, S], BF16)
        V_sb = sbuf("v_sb", [P, NT, HD], BF16)
        xb2 = Q_sb
        v_flat = V_sb[:].rearrange("p a b -> p (a b)")
        drow = sbuf("drow", [1, SB], F32)
        dbc_sb = sbuf("dbc_sb", [P, SB], F32)

        s_te = es.enter_context(nc.semaphore("s_te"))
        s_sc = es.enter_context(nc.semaphore("s_sc"))
        s_dv = es.enter_context(nc.semaphore("s_dv"))
        s_po = es.enter_context(nc.semaphore("s_po"))
        s_x = [es.enter_context(nc.semaphore(f"s_x{i}")) for i in range(4)]
        s_wk = es.enter_context(nc.semaphore("s_wk"))
        s_wv = es.enter_context(nc.semaphore("s_wv"))
        s_wa = es.enter_context(nc.semaphore("s_wa"))
        s_wb = es.enter_context(nc.semaphore("s_wb"))
        s_xc = [es.enter_context(nc.semaphore(f"s_xc{i}")) for i in range(4)]
        s_xres = es.enter_context(nc.semaphore("s_xres"))
        s_swr = es.enter_context(nc.semaphore("s_swr"))
        s_bir = es.enter_context(nc.semaphore("s_bir"))
        s_y = es.enter_context(nc.semaphore("s_y"))
        s_wq3 = es.enter_context(nc.semaphore("s_wq3"))

        def wgate(name):
            if name in WA:
                return s_wa, 16 * (WA.index(name) + 1)
            return s_wb, 16 * (WB.index(name) + 1)

        block = es.enter_context(nc.Block())

        te, sc, dv, po = Ctr(s_te), Ctr(s_sc), Ctr(s_dv), Ctr(s_po)
        n_y = 0

        def _consts(v):
            v.memset(cst[:, 0:1], EPS)
            v.memset(cst[:, 1:2], 1.0 / D)
            v.memset(cst[:, 2:3], SCALE)
            v.memset(cst[:, 3:4], -0.5)
            v.memset(cst[:, 4:5], -1.0)
            v.memset(ones_c[:], 1.0)
            v.memset(ones_rb[:], 1.0)
            dv.inc(v.memset(ones_r[:], 1.0))
        block.vector(_consts)
        M_CONST = dv.n

        x_bf = xbuf
        with nc.sbuf_tensor("wq_q", [P, 2, C, SB], BF16) as wq_q, \
             nc.sbuf_tensor("wk_s", [P, C, HD], BF16) as wk_s, \
             nc.sbuf_tensor("wv_s", [P, C, HD], BF16) as wv_s, \
             nc.sbuf_tensor("sq_scr", [P, 2, S], BF16) as sq_scr, \
             nc.sbuf_tensor("m_row", [1, S], F32) as m_row, \
             nc.sbuf_tensor("r_row", [1, S], F32) as r_row, \
             nc.sbuf_tensor("m_rowb", [1, S], BF16) as m_rowb, \
             nc.sbuf_tensor("r_rowb", [1, S], BF16) as r_rowb, \
             nc.sbuf_tensor("m_bc", [P, S], BF16) as m_bc, \
             nc.sbuf_tensor("r_bc", [P, S], BF16) as r_bc:

            xstg = Q_sb[:].bitcast(F32).rearrange("p (a c) b -> p a (c b)", a=4)
            SCC0 = sc.n
            STAGED = [5, 6, 7, 8, 9, 2, 3, 10, 11, 12, 13, 14, 15]
            SC_CASTS = STAGED[0::2]
            DV_CASTS = STAGED[1::2]

            def stg_slot(c):
                return STAGED.index(c) % 4

            def stg_wait(c):
                i = STAGED.index(c)
                return s_xc[i % 4], 16 * (i // 4 + 1)

            xcast_sc = {c: SCC0 + i + 1 for i, c in enumerate(SC_CASTS)}
            M_MROW = SCC0 + len(SC_CASTS) + 1
            M_MROWB = M_MROW + 1
            M_RSTD1 = M_MROWB + 1
            M_BCEVM, M_BCEVR = M_RSTD1 + 1, M_RSTD1 + 2
            EVAC_ORDER = ["kA", "v0", "q0", "v1", "q2", "kB", "q1", "q3", "v2",
                          "q4", "q5", "v3", "q6", "q7", "q8", "q9", "q10", "q11"]
            sc_ev = {n: M_BCEVR + 1 + i for i, n in enumerate(EVAC_ORDER)}
            SC_P1_END = M_BCEVR + len(EVAC_ORDER)

            STAT_ORDER = [5, 6, 7, 0, 1, 8, 9, 2, 4, 3, 10, 11, 12, 13, 14, 15]
            DV0 = dv.n
            xcast_dv = {}
            sqmark = {}
            _n = DV0
            for _ci, _c in enumerate(STAT_ORDER):
                if _c in DV_CASTS:
                    _n += 1; xcast_dv[_c] = _n
                _n += 1; sqmark[_ci] = _n
            M_VARROW = _n + 1
            M_RCASTD = _n + 2

            def _sy_xc(syn):
                for i, c in enumerate(STAGED):
                    if i >= 4:
                        pc = STAGED[i - 4]
                        if pc in SC_CASTS:
                            syn.wait_ge(s_sc, xcast_sc[pc])
                        else:
                            syn.wait_ge(s_dv, xcast_dv[pc])
                    syn.dma_start(out=xstg[:, i % 4, :],
                                  in_=xT_r[:, c, :]).then_inc(s_xc[i % 4], 16)
            block.sync(_sy_xc)

            def _sc_xc(s):
                s.wait_ge(s_dv, M_CONST)
                for c in SC_CASTS:
                    sm, val = stg_wait(c)
                    s.wait_ge(sm, val)
                    sc.inc(s.copy(x_bf[:, c, :], xstg[:, stg_slot(c), :]))
            block.scalar(_sc_xc)
            assert sc.n == xcast_sc[15]

            with nc.psum_tensor("st_ps", [P, S], F32) as st_ps, \
                 nc.psum_tensor("sq_ps", [P, S], F32) as sq_ps:

                assert DV0 == dv.n
                m2_scr = xstg[0:1, 0, :]

                def _te_p0(t):
                    t.wait_ge(s_dv, M_CONST)
                    for ci, c in enumerate(STAT_ORDER):
                        if c == 0:
                            t.wait_ge(s_x[0], 16)
                        elif c == 4:
                            t.wait_ge(s_x[2], 16)
                        elif c in SC_CASTS:
                            t.wait_ge(s_sc, xcast_sc[c])
                        elif c in DV_CASTS:
                            t.wait_ge(s_dv, xcast_dv[c])
                        for tb in range(4):
                            t.matmul(st_ps[0:1, tb * SB:(tb + 1) * SB], ones_c[:],
                                     x_bf[:, c, tb * SB:(tb + 1) * SB],
                                     start=(ci == 0), stop=(ci == C - 1))
                        t.wait_ge(s_dv, sqmark[ci])
                        ii = None
                        for tb in range(4):
                            ii = t.matmul(sq_ps[0:1, tb * SB:(tb + 1) * SB], ones_c[:],
                                          sq_scr[:, ci % 2, tb * SB:(tb + 1) * SB],
                                          start=(ci == 0), stop=(ci == C - 1))
                        marks["stat"][ci] = te.inc(ii)
                block.tensor(_te_p0)

                def _dv_p0a(v):
                    for ci, c in enumerate(STAT_ORDER):
                        if c in DV_CASTS:
                            sm, val = stg_wait(c)
                            v.wait_ge(sm, val)
                            dv.inc(v.tensor_copy(x_bf[:, c, :],
                                                 xstg[:, stg_slot(c), :]))
                        if c in (0, 1):
                            v.wait_ge(s_x[0], 16)
                        elif c == 4:
                            v.wait_ge(s_x[2], 16)
                        elif c in SC_CASTS:
                            v.wait_ge(s_sc, xcast_sc[c])
                        if ci >= 2:
                            v.wait_ge(s_te, marks["stat"][ci - 2])
                        dv.inc(v.tensor_mul(sq_scr[:, ci % 2, :], x_bf[:, c, :], x_bf[:, c, :]))
                        assert dv.n == sqmark[ci]
                    v.wait_ge(s_sc, M_MROW)
                    v.tensor_mul(m2_scr, m_row[:], m_row[:])
                    dv.inc(v.scalar_tensor_tensor(r_row[:], sq_ps[0:1, :],
                                                  cst[0:1, 1:2], m2_scr,
                                                  ALU.mult, ALU.subtract))
                    assert dv.n == M_VARROW
                    v.wait_ge(s_sc, M_RSTD1)
                    dv.inc(v.tensor_copy(r_rowb[:], r_row[:]))
                    assert dv.n == M_RCASTD
                block.vector(_dv_p0a)

                def _sc_rows(s):
                    s.wait_ge(s_te, marks["stat"][C - 1])
                    sc.inc(s.activation(m_row[:], st_ps[0:1, :], AF.Copy,
                                        scale=cst[0:1, 1:2]))
                    sc.inc(s.copy(m_rowb[:], m_row[:]))
                    s.wait_ge(s_dv, M_VARROW)
                    s.activation(r_row[:], r_row[:], AF.Ln, bias=cst[0:1, 0:1])
                    sc.inc(s.activation(r_row[:], r_row[:], AF.Exp, scale=cst[0:1, 3:4]))
                block.scalar(_sc_rows)
                assert sc.n == M_RSTD1

            with nc.psum_tensor("bcm_ps", [P, S], F32) as bcm_ps, \
                 nc.psum_tensor("bcr_ps", [P, S], F32) as bcr_ps:

                def _te_p0b(t):
                    t.wait_ge(s_sc, M_MROWB)
                    ii = None
                    for tb in range(4):
                        ii = t.matmul(bcm_ps[:, tb * SB:(tb + 1) * SB], ones_rb[:],
                                      m_rowb[0:1, tb * SB:(tb + 1) * SB], start=True, stop=True)
                    marks["bcm"] = te.inc(ii)
                    t.wait_ge(s_dv, M_RCASTD)
                    for tb in range(4):
                        ii = t.matmul(bcr_ps[:, tb * SB:(tb + 1) * SB], ones_rb[:],
                                      r_rowb[0:1, tb * SB:(tb + 1) * SB], start=True, stop=True)
                    marks["bcr"] = te.inc(ii)
                block.tensor(_te_p0b)

                def _sc_p0b(s):
                    s.wait_ge(s_te, marks["bcm"])
                    sc.inc(s.copy(m_bc[:], bcm_ps[:]))
                    s.wait_ge(s_te, marks["bcr"])
                    sc.inc(s.copy(r_bc[:], bcr_ps[:]))
                block.scalar(_sc_p0b)
                assert sc.n == M_BCEVR

                DVA = dv.n

                def aph0_mark(c):
                    return DVA + c + 1

                def aph1_mark(c):
                    return DVA + C + c + 1

                def _dv_p0c(v):
                    v.wait_ge(s_sc, M_BCEVM)
                    for c in range(4):
                        v.tensor_sub(x_bf[:, c, 0:SO], x_bf[:, c, 0:SO], m_bc[:, 0:SO])
                    v.wait_ge(s_sc, M_BCEVR)
                    for c in range(C):
                        if c >= 4:
                            v.tensor_sub(x_bf[:, c, 0:SO], x_bf[:, c, 0:SO], m_bc[:, 0:SO])
                        dv.inc(v.tensor_mul(x_bf[:, c, 0:SO], x_bf[:, c, 0:SO], r_bc[:, 0:SO]))
                    for c in range(C):
                        v.tensor_sub(x_bf[:, c, SO:S], x_bf[:, c, SO:S], m_bc[:, SO:S])
                        dv.inc(v.tensor_mul(x_bf[:, c, SO:S], x_bf[:, c, SO:S], r_bc[:, SO:S]))
                block.vector(_dv_p0c)
                assert dv.n == aph1_mark(C - 1)

            with nc.psum_tensor("p1_ps", [P, 4, SB], F32) as p1_ps, \
                 nc.psum_tensor("v_ps", [P, NT, HD], F32) as v_ps:

                def _te_p1(t):
                    def k_half(hh):
                        if hh == 0:
                            t.wait_ge(s_wk, 16)
                        ii = None
                        for c in range(C):
                            t.wait_ge(s_dv, aph0_mark(c) if hh == 0 else aph1_mark(c))
                            for tb in range(2):
                                ii = t.matmul(p1_ps[:, 2 * hh + tb, :], wk_s[:, c, :],
                                              x_bf[:, c, hh * SO + tb * SB:
                                                   hh * SO + (tb + 1) * SB],
                                              start=(c == 0), stop=(c == C - 1))
                        marks["kA" if hh == 0 else "kB"] = te.inc(ii)

                    def v_group(g):
                        if g == 0:
                            t.wait_ge(s_wv, 16)
                        ii = None
                        for tt in range(4 * g, 4 * g + 4):
                            for c in range(C):
                                ii = t.matmul(v_ps[:, tt, :], x_bf[:, c, tt * P:(tt + 1) * P],
                                              wv_s[:, c, :], start=(c == 0), stop=(c == C - 1))
                        marks["v"][g] = te.inc(ii)

                    def q_tile(f):
                        q, fi = f // 4, f % 4
                        if f in (0, 4, 8):
                            t.wait_ge(*wgate(f"wq{q}"))
                        if f == 0:
                            t.wait_ge(s_sc, sc_ev["kA"])
                        elif f == 1:
                            t.wait_ge(s_sc, sc_ev["kB"])
                        else:
                            t.wait_ge(s_sc, sc_ev[f"q{f - 2}"])
                        ii = None
                        for c in range(C):
                            for sb in range(NSB):
                                ii = t.matmul(p1_ps[:, 2 * (f % 2) + sb, :],
                                              wq_q[:, q % 2, c, fi * P:(fi + 1) * P],
                                              x_bf[:, c, sb * SB:(sb + 1) * SB],
                                              start=(c == 0), stop=(c == C - 1))
                        marks["q"][f] = te.inc(ii)

                    k_half(0)
                    v_group(0)
                    q_tile(0)
                    v_group(1)
                    q_tile(2)
                    k_half(1)
                    q_tile(1)
                    q_tile(3)
                    v_group(2)
                    q_tile(4)
                    q_tile(5)
                    v_group(3)
                    for f in (6, 7, 8, 9, 10, 11):
                        q_tile(f)
                    marks["p1_done"] = te.n
                block.tensor(_te_p1)

                def _sc_p1(s):
                    for name in EVAC_ORDER:
                        if name in ("kA", "kB"):
                            hh = 0 if name == "kA" else 1
                            s.wait_ge(s_te, marks[name])
                            ii = None
                            for tb in range(2):
                                ii = s.copy(K_sb[:, (2 * hh + tb) * SB:
                                                 (2 * hh + tb + 1) * SB],
                                            p1_ps[:, 2 * hh + tb, :])
                            sc.inc(ii)
                        elif name[0] == "v":
                            g = int(name[1:])
                            s.wait_ge(s_te, marks["v"][g])
                            sc.inc(s.copy(V_sb[:, 4 * g:4 * g + 4, :],
                                          v_ps[:, 4 * g:4 * g + 4, :]))
                        else:
                            f = int(name[1:])
                            s.wait_ge(s_te, marks["q"][f])
                            ii = None
                            for sb in range(NSB):
                                ii = s.copy(Q_sb[:, f, sb * SB:(sb + 1) * SB],
                                            p1_ps[:, 2 * (f % 2) + sb, :])
                            sc.inc(ii)
                        assert sc.n == sc_ev[name]
                block.scalar(_sc_p1)
                assert sc.n == SC_P1_END

            def _gp_a(g):
                g.dma_start(out=x_bf[:, 0:2, :],
                            in_=xT_r[:, 0:2, :]).then_inc(s_x[0], 16)
                g.dma_start(out=x_bf[:, 4:5, :],
                            in_=xT_r[:, 4:5, :]).then_inc(s_x[2], 16)
                g.dma_start(out=wk_s[:], in_=wk_r[:]).then_inc(s_wk, 16)
                g.dma_start(out=wv_s[:], in_=wv_r[:]).then_inc(s_wv, 16)
                for q in range(3):
                    if q >= 2:
                        g.wait_ge(s_te, marks["q"][4 * (q - 2) + 3])
                    g.dma_start(out=wq_q[:, q % 2],
                                in_=wq_r[:, :, q * SB:(q + 1) * SB]).then_inc(wgate(f"wq{q}")[0], 16)
            block.gpsimd(_gp_a)

            if debug:
                def _sy_dbg1(syn):
                    syn.wait_ge(s_dv, M_VARROW)
                    syn.wait_ge(s_sc, M_RSTD1)
                    syn.dma_start(out=dbg["m"], in_=m_row[:]).then_inc(s_y, 16)
                    syn.dma_start(out=dbg["r"], in_=r_row[:]).then_inc(s_y, 16)
                    syn.wait_ge(s_sc, SC_P1_END)
                    syn.dma_start(out=dbg["k"], in_=K_sb[:]).then_inc(s_y, 16)
                    syn.dma_start(out=dbg["v"], in_=V_sb[:]).then_inc(s_y, 16)
                    syn.dma_start(out=dbg["q"], in_=Q_sb[:]).then_inc(s_y, 16)
                block.sync(_sy_dbg1)

        if True:
            x_res = xbuf[:].bitcast(F32)
            m2_row = None
            r2_row = None

            with nc.sbuf_tensor("o_sb", [P, H, SO], BF16) as O_sb, \
                 nc.sbuf_tensor("wo_q", [P, 2, C, 256], BF16) as wo_q:

                wq3v = wo_q[:].rearrange("p a c f -> p (a c f)").rearrange(
                    "p (c f) -> p c f", f=SB)

                def _gp_wq3(g):
                    g.wait_ge(s_te, marks["p1_done"])
                    g.dma_start(out=wq3v,
                                in_=wq_r[:, :, 3 * SB:4 * SB]).then_inc(s_wq3, 16)
                block.gpsimd(_gp_wq3)

                with nc.sbuf_tensor("e_sb", [P, 2, NT * SB], BF16) as E, \
                     nc.sbuf_tensor("tsc", [P, 2, NT * SB // 2], BF16) as TSC, \
                     nc.psum_tensor("qk_ps", [P, 4, SB], F32) as qk_ps, \
                     nc.psum_tensor("pv_ps", [P, 2, SB], F32) as pv_ps, \
                     nc.psum_tensor("d_ps", [P, SB], F32) as d_ps, \
                     nc.psum_tensor("wo_ps", [P, SB], F32) as wo_ps:

                    SC2 = sc.n
                    DV2 = dv.n

                    def xcast2_mark(c):
                        return SC_P2_END + c + 1

                    expb_mark = {}
                    ln_mark = {}
                    expd_mark = {}
                    _n = SC2
                    for _i in range(NITER):
                        for _j in range(2):
                            _n += 1; expb_mark[8 * _i + _j] = _n
                        if _i >= 2:
                            _n += 1; ln_mark[_i - 2] = _n
                        for _j in (2, 3, 4, 5):
                            _n += 1; expb_mark[8 * _i + _j] = _n
                        if _i >= 2:
                            _n += 1; expd_mark[_i - 2] = _n
                        for _j in (6, 7):
                            _n += 1; expb_mark[8 * _i + _j] = _n
                    _n += 1; ln_mark[30] = _n
                    _n += 1; expd_mark[30] = _n
                    _n += 1; ln_mark[31] = _n
                    _n += 1; expd_mark[31] = _n
                    SC_P2_END = _n

                    tree_mark = {}
                    dsp_mark = {}
                    onorm_mark = {}
                    woep_mark = {}
                    qev2_mark = {}
                    _n = DV2
                    for _i in range(NITER):
                        if 2 <= _i <= 16 and _i % 2 == 0:
                            _n += 1; qev2_mark[(_i - 2) // 2] = _n
                        if _i >= 19:
                            _n += 1; woep_mark[_i - 19] = _n
                        if _i >= 2:
                            _n += 1; dsp_mark[_i - 2] = _n
                        if _i >= 1:
                            _n += 1; tree_mark[_i - 1] = _n
                        if _i >= 2:
                            _n += 1; onorm_mark[_i - 2] = _n
                    sq2_mark = {}
                    _n += 1; woep_mark[13] = _n
                    _n += 1; tree_mark[31] = _n
                    _n += 1; dsp_mark[30] = _n
                    _n += 1; onorm_mark[30] = _n
                    _n += 1; woep_mark[14] = _n
                    _n += 1; dsp_mark[31] = _n
                    _n += 1; onorm_mark[31] = _n
                    for _k in range(15, 32):
                        _n += 1; woep_mark[_k] = _n
                        if _k >= 16:
                            _n += 1; sq2_mark[_k - 16] = _n
                    DV_P2_END = _n

                    def wo_gp_gate(k):
                        rnd, p = k // 16, (k % 16) // 2
                        return wgate(f"wo{p}" if rnd == 0 else f"wob{p}")

                    def wo_lhsT(k, h):
                        p = (k % 16) // 2
                        f = k % 16
                        return wo_q[:, p % 2, h, (f % 2) * P:(f % 2 + 1) * P]

                    def _te_wo_group(t, k, h0, h1):
                        sbw = k // 16
                        out_ap = wo_ps[:] if k <= 15 else qk_ps[:, k % 4, :]
                        ii = None
                        for h in range(h0, h1):
                            ii = t.matmul(out_ap, wo_lhsT(k, h),
                                          O_sb[:, h, sbw * SB:(sbw + 1) * SB],
                                          start=(h == 0), stop=(h == H - 1))
                        return ii

                    def _te_p2(t):
                        t.wait_ge(s_dv, DV2)
                        t.wait_ge(s_sc, SC_P1_END)
                        for i in range(NITER):
                            sb, h = i // 16, i % 16
                            pj = (i - 1) % 2
                            qsl = Q_sb[:, h, sb * SB:(sb + 1) * SB]

                            def qk_batch(j, i=i, qsl=qsl):
                                g = 8 * i + j
                                if g >= 2:
                                    t.wait_ge(s_sc, expb_mark[g - 2])
                                ii = None
                                for u in range(2):
                                    c = 2 * j + u
                                    ii = t.matmul(qk_ps[:, c % 4, :],
                                                  K_sb[:, c * P:(c + 1) * P], qsl,
                                                  start=True, stop=True)
                                marks["qk"][g] = te.inc(ii)

                            def pv_part(c0, c1, i=i, pj=pj):
                                ii = None
                                for c in range(c0, c1):
                                    ii = t.matmul(pv_ps[:, pj, :], V_sb[:, c, :],
                                                  E[:, pj, c * SB:(c + 1) * SB],
                                                  start=(c == 0), stop=(c == NT - 1))
                                if c1 == NT:
                                    marks["pv"][i - 1] = te.inc(ii)

                            if i >= 2:
                                t.wait_ge(s_dv, tree_mark[i - 2])
                                if i >= 3:
                                    t.wait_ge(s_sc, expd_mark[i - 3])
                                marks["dones"][i - 2] = te.inc(
                                    t.matmul(d_ps[0:1, :], ones_c[:],
                                             TSC[:, (i - 2) % 2, 0:SB], start=True, stop=True))
                            if h >= 12:
                                t.wait_ge(s_dv, qev2_mark[2 * (h - 12)])
                                t.wait_ge(s_dv, qev2_mark[2 * (h - 12) + sb])
                            qk_batch(0)
                            qk_batch(1)
                            if i >= 1:
                                t.wait_ge(s_sc, expb_mark[8 * (i - 1) + 7])
                                if i >= 3:
                                    t.wait_ge(s_dv, onorm_mark[i - 3])
                                pv_part(0, 4)
                            qk_batch(2)
                            if i >= 1:
                                pv_part(4, 8)
                            qk_batch(3)
                            if i >= 1:
                                pv_part(8, 12)
                            qk_batch(4)
                            if i >= 1:
                                pv_part(12, 16)
                            qk_batch(5)
                            if i >= 2:
                                t.wait_ge(s_dv, dsp_mark[i - 2])
                                t.matmul(d_ps[:], ones_rb[:], drow_hi[:],
                                         start=True, stop=False)
                                marks["dbc"][i - 2] = te.inc(
                                    t.matmul(d_ps[:], ones_rb[:], drow_lo[:],
                                             start=False, stop=True))
                            qk_batch(6)
                            if i >= 18:
                                k = i - 18
                                if k == 0:
                                    t.wait_ge(s_dv, onorm_mark[15])
                                else:
                                    t.wait_ge(s_dv, woep_mark[k - 1])
                                if k % 2 == 0:
                                    t.wait_ge(*wo_gp_gate(k))
                                _te_wo_group(t, k, 0, 8)
                            qk_batch(7)
                            if i >= 18:
                                marks["wo"][i - 18] = te.inc(_te_wo_group(t, i - 18, 8, H))
                            if i <= 15:
                                g2, hh = i // 2, i % 2
                                f2, sb2 = 12 + g2 // 2, g2 % 2
                                if i == 0:
                                    t.wait_ge(s_wq3, 16)
                                if hh == 0 and g2 >= 1:
                                    t.wait_ge(s_dv, qev2_mark[g2 - 1])
                                ii = None
                                for c in range(8 * hh, 8 * hh + 8):
                                    ii = t.matmul(wo_ps[:],
                                                  wq3v[:, c, (f2 - 12) * P:(f2 - 11) * P],
                                                  x_bf[:, c, sb2 * SB:(sb2 + 1) * SB],
                                                  start=(c == 0), stop=(c == C - 1))
                                if hh == 1:
                                    marks["q2"][g2] = te.inc(ii)
                        t.wait_ge(s_sc, expb_mark[8 * NITER - 1])
                        t.wait_ge(s_dv, onorm_mark[NITER - 3])
                        pj = (NITER - 1) % 2
                        ii = None
                        for c in range(NT):
                            ii = t.matmul(pv_ps[:, pj, :], V_sb[:, c, :],
                                          E[:, pj, c * SB:(c + 1) * SB],
                                          start=(c == 0), stop=(c == NT - 1))
                        marks["pv"][NITER - 1] = te.inc(ii)

                        def tail_d(j2):
                            t.wait_ge(s_dv, tree_mark[j2])
                            t.wait_ge(s_sc, expd_mark[j2 - 1])
                            marks["dones"][j2] = te.inc(
                                t.matmul(d_ps[0:1, :], ones_c[:], TSC[:, j2 % 2, 0:SB],
                                         start=True, stop=True))

                        def tail_dbc(j2):
                            t.wait_ge(s_dv, dsp_mark[j2])
                            t.matmul(d_ps[:], ones_rb[:], drow_hi[:],
                                     start=True, stop=False)
                            marks["dbc"][j2] = te.inc(
                                t.matmul(d_ps[:], ones_rb[:], drow_lo[:],
                                         start=False, stop=True))

                        def tail_wo(k):
                            if k == 14:
                                t.wait_ge(s_dv, woep_mark[13])
                            elif k == 15:
                                t.wait_ge(s_dv, woep_mark[14])
                            else:
                                if k == 16:
                                    t.wait_ge(s_dv, onorm_mark[NITER - 1])
                                if k >= 20:
                                    t.wait_ge(s_dv, woep_mark[k - 4])
                            if k % 2 == 0:
                                t.wait_ge(*wo_gp_gate(k))
                            marks["wo"][k] = te.inc(_te_wo_group(t, k, 0, H))

                        def stats2_c(c):
                            t.wait_ge(s_dv, sq2_mark[c])
                            for tb in range(2):
                                t.matmul(pv_ps[0:1, tb, :], ones_c[:],
                                         xb2[:, c, tb * SB:(tb + 1) * SB],
                                         start=(c == 0), stop=(c == C - 1))
                            ii = None
                            for tb in range(2):
                                dst = d_ps[0:1, :] if tb == 0 else wo_ps[0:1, :]
                                ii = t.matmul(dst, ones_c[:],
                                              v_flat[:, (c % 2) * SO + tb * SB:
                                                     (c % 2) * SO + (tb + 1) * SB],
                                              start=(c == 0), stop=(c == C - 1))
                            marks["stat2"][c] = te.inc(ii)

                        tail_d(30)
                        tail_wo(14)
                        tail_dbc(30)
                        tail_d(31)
                        tail_wo(15)
                        tail_dbc(31)
                        for k in range(16, 32):
                            tail_wo(k)
                            if k == 18:
                                t.wait_ge(s_dv, onorm_mark[NITER - 1])
                                t.wait_ge(s_sc, expd_mark[NITER - 1])
                            if k >= 18:
                                stats2_c(k - 18)
                        stats2_c(14)
                        stats2_c(15)
                    block.tensor(_te_p2)

                    def _sy_xres(syn):
                        syn.wait_ge(s_te, marks["q2"][7])
                        syn.dma_start(out=x_res[:],
                                      in_=xT_r[:, :, 0:SO]).then_inc(s_xres, 16)
                    block.sync(_sy_xres)

                    def _sc_p2(s):
                        def d_ln(jj):
                            s.wait_ge(s_te, marks["dones"][jj])
                            if jj >= 1:
                                s.wait_ge(s_dv, dsp_mark[jj - 1])
                            sc.inc(s.activation(drow[0:1, :], d_ps[0:1, :], AF.Ln))
                            assert sc.n == ln_mark[jj]

                        def d_exp(jj):
                            s.wait_ge(s_te, marks["dbc"][jj])
                            if jj >= 1:
                                s.wait_ge(s_dv, onorm_mark[jj - 1])
                            sc.inc(s.activation(dbc_sb[:], d_ps[:], AF.Exp,
                                                scale=cst[:, 4:5]))
                            assert sc.n == expd_mark[jj]

                        for i in range(NITER):
                            pi = i % 2
                            if i >= 2:
                                s.wait_ge(s_te, marks["pv"][i - 2])
                                s.wait_ge(s_dv, tree_mark[i - 2])

                            def batch(j, i=i, pi=pi):
                                g = 8 * i + j
                                s.wait_ge(s_te, marks["qk"][g])
                                r = j % 2
                                sc.inc(s.activation(E[:, pi, (2 * j) * SB:(2 * j + 2) * SB],
                                                    qk_ps[:, 2 * r:2 * r + 2, :],
                                                    AF.Exp, scale=cst[:, 2:3]))
                                assert sc.n == expb_mark[g]

                            batch(0)
                            batch(1)
                            if i >= 2:
                                d_ln(i - 2)
                            for j in (2, 3, 4, 5):
                                batch(j)
                            if i >= 2:
                                d_exp(i - 2)
                            batch(6)
                            batch(7)
                        d_ln(30)
                        d_exp(30)
                        d_ln(31)
                        d_exp(31)
                        assert sc.n == SC_P2_END
                    block.scalar(_sc_p2)

                    def _po_p2(g):
                        for p in range(2):
                            g.wait_ge(s_te, marks["q2"][7])
                            g.dma_start(out=wo_q[:, p % 2],
                                        in_=wo_r[:, :, p * 256:(p + 1) * 256]).then_inc(
                                            wgate(f"wo{p}")[0], 16)
                        for p in range(2, 8):
                            g.wait_ge(s_te, marks["wo"][2 * (p - 2) + 1])
                            g.dma_start(out=wo_q[:, p % 2],
                                        in_=wo_r[:, :, p * 256:(p + 1) * 256]).then_inc(
                                            wgate(f"wo{p}")[0], 16)
                    block.gpsimd(_po_p2)

                    def _dv_p2(v):
                        half = NT * SB // 2

                        def wo_ep(k):
                            sbw, f = k // 16, k % 16
                            if k == 0:
                                v.wait_ge(s_xres, 16)
                            v.wait_ge(s_te, marks["wo"][k])
                            src_ap = wo_ps[:] if k <= 15 else qk_ps[:, k % 4, :]
                            dv.inc(v.tensor_add(x_res[:, f, sbw * SB:(sbw + 1) * SB],
                                                x_res[:, f, sbw * SB:(sbw + 1) * SB],
                                                src_ap))
                            assert dv.n == woep_mark[k]

                        def tree(jj):
                            pk = jj % 2
                            v.wait_ge(s_sc, expb_mark[8 * jj + 7])
                            if jj >= 2:
                                v.wait_ge(s_te, marks["dones"][jj - 2])
                            v.tensor_add(TSC[:, pk, :], E[:, pk, 0:half],
                                         E[:, pk, half:2 * half])
                            v.tensor_add(TSC[:, pk, 0:half // 2], TSC[:, pk, 0:half // 2],
                                         TSC[:, pk, half // 2:half])
                            v.tensor_add(TSC[:, pk, 0:half // 4], TSC[:, pk, 0:half // 4],
                                         TSC[:, pk, half // 4:half // 2])
                            dv.inc(v.tensor_add(TSC[:, pk, 0:half // 8], TSC[:, pk, 0:half // 8],
                                                TSC[:, pk, half // 8:half // 4]))
                            assert dv.n == tree_mark[jj]

                        def dsplit(jj):
                            v.wait_ge(s_sc, ln_mark[jj])
                            v.tensor_copy(drow_hi[:], drow[0:1, :])
                            dv.inc(v.tensor_sub(drow_lo[:], drow[0:1, :], drow_hi[:]))
                            assert dv.n == dsp_mark[jj]

                        def onorm(jj):
                            pk = jj % 2
                            sbp, hp = jj // 16, jj % 16
                            v.wait_ge(s_sc, expd_mark[jj])
                            dv.inc(v.tensor_mul(O_sb[:, hp, sbp * SB:(sbp + 1) * SB],
                                                pv_ps[:, pk, :], dbc_sb[:]))
                            assert dv.n == onorm_mark[jj]

                        for i in range(NITER):
                            if 2 <= i <= 16 and i % 2 == 0:
                                g2 = (i - 2) // 2
                                f2, sb2 = 12 + g2 // 2, g2 % 2
                                v.wait_ge(s_te, marks["q2"][g2])
                                dv.inc(v.tensor_copy(
                                    Q_sb[:, f2, sb2 * SB:(sb2 + 1) * SB], wo_ps[:]))
                                assert dv.n == qev2_mark[g2]
                            if i >= 19:
                                wo_ep(i - 19)
                            if i >= 2:
                                dsplit(i - 2)
                            if i >= 1:
                                tree(i - 1)
                            if i >= 2:
                                onorm(i - 2)
                        def sq2(c):
                            v.wait_ge(s_sc, xcast2_mark(c))
                            if c >= 2:
                                v.wait_ge(s_te, marks["stat2"][c - 2])
                            dv.inc(v.tensor_mul(v_flat[:, (c % 2) * SO:(c % 2 + 1) * SO],
                                                xb2[:, c, :], xb2[:, c, :]))
                            assert dv.n == sq2_mark[c]

                        wo_ep(13)
                        tree(31)
                        dsplit(30)
                        onorm(30)
                        wo_ep(14)
                        dsplit(31)
                        onorm(31)
                        for k in range(15, 32):
                            wo_ep(k)
                            if k >= 16:
                                sq2(k - 16)
                        assert dv.n == DV_P2_END
                    block.vector(_dv_p2)

                    def _gp_b(g):
                        for p in range(8):
                            if p < 2:
                                g.wait_ge(s_te, marks["wo"][13 if p == 0 else 15])
                            else:
                                g.wait_ge(s_te, marks["wo"][16 + 2 * (p - 2) + 1])
                            g.dma_start(out=wo_q[:, p % 2],
                                        in_=wo_r[:, :, p * 256:(p + 1) * 256]).then_inc(
                                            wgate(f"wob{p}")[0], 16)
                    block.gpsimd(_gp_b)

                    def _sc_p4cast(s):
                        for c in range(C):
                            s.wait_ge(s_dv, woep_mark[16 + c])
                            sc.inc(s.copy(xb2[:, c, :], x_res[:, c, :]))
                    block.scalar(_sc_p4cast)
                    assert sc.n == xcast2_mark(C - 1)

                    pv01 = pv_ps[0:1, :, :].rearrange("p a b -> p (a b)")
                    _tscf = TSC[:].bitcast(F32).rearrange("p a b -> p (a b)")
                    _tscb = TSC[:].rearrange("p a b -> p (a b)")
                    m2_row = _tscf[0:1, 0:SO]
                    r2_row = _tscf[0:1, SO:2 * SO]
                    m2_rowb = _tscb[0:1, 4 * SO:5 * SO]
                    r2_rowb = _tscb[0:1, 5 * SO:6 * SO]

                    m2sq_row = _tscf[0:1, 3 * SO:4 * SO]

                    def _dv_m2(v):
                        v.wait_ge(s_te, marks["stat2"][C - 1])
                        v.tensor_scalar_mul(m2_row, pv01, cst[0:1, 1:2])
                        v.tensor_copy(m2_rowb, m2_row)
                        v.tensor_mul(m2sq_row, m2_row, m2_row)
                        v.scalar_tensor_tensor(_tscf[0:1, SO:SO + SB], d_ps[0:1, :],
                                               cst[0:1, 1:2],
                                               _tscf[0:1, 3 * SO:3 * SO + SB],
                                               ALU.mult, ALU.subtract)
                        dv.inc(v.scalar_tensor_tensor(
                            _tscf[0:1, SO + SB:2 * SO], wo_ps[0:1, :],
                            cst[0:1, 1:2], _tscf[0:1, 3 * SO + SB:4 * SO],
                            ALU.mult, ALU.subtract))
                    block.vector(_dv_m2)
                    M_VARROW2 = dv.n

                    def _sc_m2(s):
                        s.wait_ge(s_dv, M_VARROW2)
                        s.activation(r2_row, r2_row, AF.Ln, bias=cst[0:1, 0:1])
                        sc.inc(s.activation(r2_row, r2_row, AF.Exp,
                                            scale=cst[0:1, 3:4]))
                    block.scalar(_sc_m2)
                    M_RSTD2 = sc.n

                    def _dv_r2cast(v):
                        v.wait_ge(s_sc, M_RSTD2)
                        dv.inc(v.tensor_copy(r2_rowb, r2_row))
                    block.vector(_dv_r2cast)
                    M_R2CAST = dv.n

                if debug:
                    def _sy_dbg2(syn):
                        syn.wait_ge(s_dv, onorm_mark[NITER - 1])
                        syn.dma_start(out=dbg["o"], in_=O_sb[:]).then_inc(s_y, 16)
                        syn.wait_ge(s_dv, DV_P2_END)
                        syn.dma_start(out=dbg["x1"], in_=x_res[:]).then_inc(s_y, 16)
                    block.sync(_sy_dbg2)

            with nc.sbuf_tensor("wt_q", [P, 2, C, 256], BF16) as wt_q, \
                 nc.sbuf_tensor("m2_bc", [P, SO], BF16) as m2_bc, \
                 nc.sbuf_tensor("r2_bc", [P, SO], BF16) as r2_bc, \
                 nc.sbuf_tensor("sw_row", [1, D], F32) as sw_row, \
                 nc.sbuf_tensor("bias_row", [1, D], F32) as bias_row, \
                 nc.sbuf_tensor("sw_col", [P, C], F32) as sw_col, \
                 nc.sbuf_tensor("bias_col", [P, C], F32) as bias_col, \
                 nc.sbuf_tensor("t1_scr", [P, 2, SB], F32) as t1_scr:

                DV4 = dv.n

                with nc.psum_tensor("bc2m_ps", [P, SO], F32) as bc2m_ps, \
                     nc.psum_tensor("bc2r_ps", [P, SO], F32) as bc2r_ps, \
                     nc.psum_tensor("swc_ps", [P, C], F32) as swc_ps, \
                     nc.psum_tensor("bic_ps", [P, C], F32) as bic_ps:

                    def _te_p4b(t):
                        t.wait_ge(s_dv, M_R2CAST)
                        ii = None
                        for tb in range(2):
                            t.matmul(bc2m_ps[:, tb * SB:(tb + 1) * SB], ones_rb[:],
                                     m2_rowb[:, tb * SB:(tb + 1) * SB], start=True, stop=True)
                            ii = t.matmul(bc2r_ps[:, tb * SB:(tb + 1) * SB], ones_rb[:],
                                          r2_rowb[:, tb * SB:(tb + 1) * SB], start=True, stop=True)
                        marks["bc2"] = te.inc(ii)
                        t.wait_ge(s_swr, 16)
                        t.wait_ge(s_bir, 16)
                        for c in range(C):
                            t.matmul(swc_ps[:, c:c + 1], sw_row[0:1, c * P:(c + 1) * P],
                                     ones_r[0:1, 0:1], start=(c == 0), stop=(c == C - 1))
                        ii = None
                        for c in range(C):
                            ii = t.matmul(bic_ps[:, c:c + 1], bias_row[0:1, c * P:(c + 1) * P],
                                          ones_r[0:1, 0:1], start=(c == 0), stop=(c == C - 1))
                        marks["cols"] = te.inc(ii)
                    block.tensor(_te_p4b)

                    def _sc_p4ev(s):
                        s.wait_ge(s_te, marks["bc2"])
                        sc.inc(s.copy(m2_bc[:], bc2m_ps[:]))
                        sc.inc(s.copy(r2_bc[:], bc2r_ps[:]))
                    block.scalar(_sc_p4ev)
                    M_BC2EV = sc.n

                    DVB = dv.n

                    def apply2_mark(c):
                        return DVB + 1 + c + 1

                    def _dv_p4c(v):
                        v.wait_ge(s_te, marks["cols"])
                        v.tensor_copy(sw_col[:], swc_ps[:])
                        dv.inc(v.tensor_copy(bias_col[:], bic_ps[:]))
                        v.wait_ge(s_sc, M_BC2EV)
                        for c in range(C):
                            v.tensor_sub(xb2[:, c, :], xb2[:, c, :], m2_bc[:])
                            dv.inc(v.tensor_mul(xb2[:, c, :], xb2[:, c, :], r2_bc[:]))
                    block.vector(_dv_p4c)
                    M_COLS = dv.n
                    assert M_COLS == apply2_mark(C - 1)

                with nc.psum_tensor("ffn_ps", [P, 4, SB], F32) as ffn_ps:
                    DVF = dv.n

                    def ffep_mark(k):
                        return DVF + k + 1

                    def _te_p4c(t):
                        for g in range(8):
                            for seg in range(4):
                                for k in range(4 * g, 4 * g + 4):
                                    o, sb = k // 2, k % 2
                                    if seg == 0:
                                        if k % 4 == 0:
                                            t.wait_ge(*wgate(f"wt{g}"))
                                        if k >= 4:
                                            t.wait_ge(s_dv, ffep_mark(k - 4))
                                    if g == 0 and k == 0:
                                        t.wait_ge(s_dv, apply2_mark(4 * seg + 3))
                                    ii = None
                                    for c in range(4 * seg, 4 * seg + 4):
                                        ii = t.matmul(ffn_ps[:, k % 4, :],
                                                      wt_q[:, g % 2, c, (o % 2) * P:(o % 2 + 1) * P],
                                                      xb2[:, c, sb * SB:(sb + 1) * SB],
                                                      start=(c == 0), stop=(c == C - 1))
                                    if seg == 3:
                                        marks["ffn"][k] = te.inc(ii)
                    block.tensor(_te_p4c)

                    def _dv_p4d(v):
                        for k in range(32):
                            o, sb = k // 2, k % 2
                            v.wait_ge(s_te, marks["ffn"][k])
                            v.tensor_scalar(t1_scr[:, k % 2, :], ffn_ps[:, k % 4, :],
                                            sw_col[:, o:o + 1], bias_col[:, o:o + 1],
                                            ALU.mult, ALU.add)
                            dv.inc(v.tensor_add(x_res[:, o, sb * SB:(sb + 1) * SB],
                                                x_res[:, o, sb * SB:(sb + 1) * SB],
                                                t1_scr[:, k % 2, :]))
                    block.vector(_dv_p4d)
                    M_FFN_DONE = dv.n
                    assert M_FFN_DONE == ffep_mark(31)

                def _gp_c(g):
                    for q in range(8):
                        if q < 2:
                            g.wait_ge(s_te, marks["wo"][31])
                        else:
                            g.wait_ge(s_te, marks["ffn"][4 * (q - 2) + 3])
                        g.dma_start(out=wt_q[:, q % 2],
                                    in_=wt_r[:, :, q * 256:(q + 1) * 256]).then_inc(
                                        wgate(f"wt{q}")[0], 16)
                block.gpsimd(_gp_c)

                def _sy_b(syn):
                    syn.wait_ge(s_te, marks["wo"][31])
                    syn.dma_start(out=sw_row[:], in_=sw[:]).then_inc(s_swr, 16)
                    syn.dma_start(out=bias_row[:], in_=bias[:]).then_inc(s_bir, 16)
                    ny = 7 * 16 if debug else 0
                    for jy in range(8):
                        syn.wait_ge(s_dv, ffep_mark(4 * jy + 3))
                        syn.dma_start(out=y_r[:, 2 * jy:2 * jy + 2, :],
                                      in_=x_res[:, 2 * jy:2 * jy + 2, :]).then_inc(s_y, 16)
                        ny += 16
                    syn.wait_ge(s_y, ny)
                block.sync(_sy_b)

    return nc


_NC_CACHE = {}


def kernel(L0_attn_Q_input, L0_attn_Wq, L0_attn_Wk, L0_attn_Wv, L0_attn_Wo,
           L0_ffn_weight, L0_ffn_scale_weight, L0_ffn_bias):
    x = np.asarray(L0_attn_Q_input, dtype=np.float32)
    wq_h = np.ascontiguousarray(np.asarray(L0_attn_Wq, dtype=np.float32))
    wk_h = np.ascontiguousarray(np.asarray(L0_attn_Wk, dtype=np.float32))
    wv_h = np.ascontiguousarray(np.asarray(L0_attn_Wv, dtype=np.float32))
    wo_h = np.ascontiguousarray(np.asarray(L0_attn_Wo, dtype=np.float32))
    wt_h = np.ascontiguousarray(np.asarray(L0_ffn_weight, dtype=np.float32).T)
    sw_h = np.ascontiguousarray(np.asarray(L0_ffn_scale_weight, dtype=np.float32).reshape(1, D))
    bi_h = np.ascontiguousarray(np.asarray(L0_ffn_bias, dtype=np.float32).reshape(1, D))

    if "nc" not in _NC_CACHE:
        _NC_CACHE["nc"] = build_nc(debug=False)
    nc = _NC_CACHE["nc"]

    in_maps = []
    for core in range(8):
        b, half = core // 2, core % 2
        xTb = x[b].T
        if half == 0:
            xp = np.ascontiguousarray(xTb)
        else:
            xp = np.ascontiguousarray(np.concatenate([xTb[:, SO:], xTb[:, :SO]], axis=1))
        in_maps.append({"xT": xp, "wq": wq_h, "wk": wk_h, "wv": wv_h,
                        "wo": wo_h, "wt": wt_h, "sw": sw_h, "bias": bi_h})

    res = run_bass_kernel_spmd(nc, in_maps, core_ids=list(range(8)))

    out = np.empty((4, S, D), dtype=np.float32)
    for core in range(8):
        b, half = core // 2, core % 2
        shard = np.asarray(res.results[core]["y"])
        out[b, half * SO:(half + 1) * SO, :] = shard.T
    return out


# version 27
# speedup vs baseline: 1.0319x; 1.0319x over previous
import math
from contextlib import ExitStack

import numpy as np
import concourse.bass as bass
import concourse.mybir as mybir
from concourse.bass_utils import run_bass_kernel_spmd

F32 = mybir.dt.float32
BF16 = mybir.dt.bfloat16
AF = mybir.ActivationFunctionType
ALU = mybir.AluOpType

P = 128
D = 2048
C = 16
S = 2048
SO = 1024
H = 16
HD = 128
SB = 512
NSB = SO // SB
NT = S // P
EPS = 1e-5
SCALE = 1.0 / math.sqrt(HD)
NITER = H * NSB


class Ctr:
    def __init__(self, sem):
        self.sem = sem
        self.n = 0

    def inc(self, instr, k=1):
        instr.then_inc(self.sem, k)
        self.n += k
        return self.n


def build_nc(debug=False):
    nc = bass.Bass("TRN2", target_bir_lowering=False)

    xT = nc.dram_tensor("xT", [D, S], F32, kind="ExternalInput").ap()
    wq = nc.dram_tensor("wq", [D, D], F32, kind="ExternalInput").ap()
    wk = nc.dram_tensor("wk", [D, HD], F32, kind="ExternalInput").ap()
    wv = nc.dram_tensor("wv", [D, HD], F32, kind="ExternalInput").ap()
    wo = nc.dram_tensor("wo", [D, D], F32, kind="ExternalInput").ap()
    wt = nc.dram_tensor("wt", [D, D], F32, kind="ExternalInput").ap()
    sw = nc.dram_tensor("sw", [1, D], F32, kind="ExternalInput").ap()
    bias = nc.dram_tensor("bias", [1, D], F32, kind="ExternalInput").ap()
    y = nc.dram_tensor("y", [D, SO], F32, kind="ExternalOutput").ap()

    dbg = {}
    if debug:
        dbg["k"] = nc.dram_tensor("dbg_k", [P, S], BF16, kind="ExternalOutput").ap()
        dbg["v"] = nc.dram_tensor("dbg_v", [P, NT, HD], BF16, kind="ExternalOutput").ap()
        dbg["q"] = nc.dram_tensor("dbg_q", [P, H, SO], BF16, kind="ExternalOutput").ap()
        dbg["m"] = nc.dram_tensor("dbg_m", [1, S], F32, kind="ExternalOutput").ap()
        dbg["r"] = nc.dram_tensor("dbg_r", [1, S], F32, kind="ExternalOutput").ap()
        dbg["o"] = nc.dram_tensor("dbg_o", [P, H, SO], BF16, kind="ExternalOutput").ap()
        dbg["x1"] = nc.dram_tensor("dbg_x1", [P, C, SO], F32, kind="ExternalOutput").ap()

    xT_r = xT.rearrange("(c p) s -> p c s", p=P)
    wq_r = wq.rearrange("(c p) f -> p c f", p=P)
    wk_r = wk.rearrange("(c p) h -> p c h", p=P)
    wv_r = wv.rearrange("(c p) h -> p c h", p=P)
    wo_r = wo.rearrange("(c p) f -> p c f", p=P)
    wt_r = wt.rearrange("(c p) f -> p c f", p=P)
    y_r = y.rearrange("(c p) s -> p c s", p=P)

    WA = ["wq0", "wq2", "wo0", "wo2", "wo4", "wo6",
          "wob0", "wob2", "wob4", "wob6", "wt0", "wt2", "wt4", "wt6"]
    WB = ["wq1", "wo1", "wo3", "wo5", "wo7",
          "wob1", "wob3", "wob5", "wob7", "wt1", "wt3", "wt5", "wt7"]

    marks = {"stat": [None] * C, "v": [None] * 4, "q": [None] * C,
             "qk": [None] * (NITER * 8), "pv": [None] * NITER,
             "dones": [None] * NITER, "dbc": [None] * NITER,
             "wo": [None] * 32, "stat2": [None] * C, "ffn": [None] * 32,
             "q2": [None] * 8}

    with ExitStack() as es:
        def sbuf(*a):
            return es.enter_context(nc.sbuf_tensor(*a))
        cst = sbuf("cst", [P, 6], F32)
        ones_c = sbuf("ones_c", [P, 1], BF16)
        ones_r = sbuf("ones_r", [1, P], F32)
        ones_rb = sbuf("ones_rb", [1, P], BF16)
        drow_hi = sbuf("drow_hi", [1, SB], BF16)
        drow_lo = sbuf("drow_lo", [1, SB], BF16)
        xbuf = sbuf("xbuf", [P, C, S], BF16)
        Q_sb = sbuf("q_sb", [P, H, SO], BF16)
        K_sb = sbuf("k_sb", [P, S], BF16)
        V_sb = sbuf("v_sb", [P, NT, HD], BF16)
        xb2 = Q_sb
        v_flat = V_sb[:].rearrange("p a b -> p (a b)")
        drow = sbuf("drow", [1, SB], F32)
        dbc_sb = sbuf("dbc_sb", [P, SB], F32)

        s_te = es.enter_context(nc.semaphore("s_te"))
        s_sc = es.enter_context(nc.semaphore("s_sc"))
        s_dv = es.enter_context(nc.semaphore("s_dv"))
        s_po = es.enter_context(nc.semaphore("s_po"))
        s_x = [es.enter_context(nc.semaphore(f"s_x{i}")) for i in range(4)]
        s_wk = es.enter_context(nc.semaphore("s_wk"))
        s_wv = es.enter_context(nc.semaphore("s_wv"))
        s_wa = es.enter_context(nc.semaphore("s_wa"))
        s_wb = es.enter_context(nc.semaphore("s_wb"))
        s_xc = [es.enter_context(nc.semaphore(f"s_xc{i}")) for i in range(4)]
        s_xres = es.enter_context(nc.semaphore("s_xres"))
        s_swr = es.enter_context(nc.semaphore("s_swr"))
        s_bir = es.enter_context(nc.semaphore("s_bir"))
        s_y = es.enter_context(nc.semaphore("s_y"))
        s_wq3 = es.enter_context(nc.semaphore("s_wq3"))

        def wgate(name):
            if name in WA:
                return s_wa, 16 * (WA.index(name) + 1)
            return s_wb, 16 * (WB.index(name) + 1)

        block = es.enter_context(nc.Block())

        te, sc, dv, po = Ctr(s_te), Ctr(s_sc), Ctr(s_dv), Ctr(s_po)
        n_y = 0

        def _consts(v):
            v.memset(cst[:, 0:1], EPS)
            v.memset(cst[:, 1:2], 1.0 / D)
            v.memset(cst[:, 2:3], SCALE)
            v.memset(cst[:, 3:4], -0.5)
            v.memset(cst[:, 4:5], -1.0)
            v.memset(ones_c[:], 1.0)
            v.memset(ones_rb[:], 1.0)
            dv.inc(v.memset(ones_r[:], 1.0))
        block.vector(_consts)
        M_CONST = dv.n

        x_bf = xbuf
        with nc.sbuf_tensor("wq_q", [P, 2, C, SB], BF16) as wq_q, \
             nc.sbuf_tensor("wk_s", [P, C, HD], BF16) as wk_s, \
             nc.sbuf_tensor("wv_s", [P, C, HD], BF16) as wv_s, \
             nc.sbuf_tensor("sq_scr", [P, 2, S], BF16) as sq_scr, \
             nc.sbuf_tensor("m_row", [1, S], F32) as m_row, \
             nc.sbuf_tensor("r_row", [1, S], F32) as r_row, \
             nc.sbuf_tensor("m_rowb", [1, S], BF16) as m_rowb, \
             nc.sbuf_tensor("r_rowb", [1, S], BF16) as r_rowb, \
             nc.sbuf_tensor("m_bc", [P, S], BF16) as m_bc, \
             nc.sbuf_tensor("r_bc", [P, S], BF16) as r_bc:

            xstg = Q_sb[:].bitcast(F32).rearrange("p (a c) b -> p a (c b)", a=4)
            SCC0 = sc.n
            STAGED = [5, 6, 7, 8, 9, 2, 3, 10, 11, 12, 13, 14, 15]
            SC_CASTS = STAGED[0::2]
            DV_CASTS = STAGED[1::2]

            def stg_slot(c):
                return STAGED.index(c) % 4

            def stg_wait(c):
                i = STAGED.index(c)
                return s_xc[i % 4], 16 * (i // 4 + 1)

            xcast_sc = {c: SCC0 + i + 1 for i, c in enumerate(SC_CASTS)}
            M_MROW = SCC0 + len(SC_CASTS) + 1
            M_MROWB = M_MROW + 1
            M_RSTD1 = M_MROWB + 1
            M_BCEVM, M_BCEVR = M_RSTD1 + 1, M_RSTD1 + 2
            EVAC_ORDER = ["kA", "v0", "q0", "v1", "q2", "kB", "q1", "q3", "v2",
                          "q4", "q5", "v3", "q6", "q7", "q8", "q9", "q10", "q11"]
            sc_ev = {n: M_BCEVR + 1 + i for i, n in enumerate(EVAC_ORDER)}
            SC_P1_END = M_BCEVR + len(EVAC_ORDER)

            STAT_ORDER = [5, 6, 7, 0, 1, 8, 9, 2, 4, 3, 10, 11, 12, 13, 14, 15]
            DV0 = dv.n
            xcast_dv = {}
            sqmark = {}
            _n = DV0
            for _ci, _c in enumerate(STAT_ORDER):
                if _c in DV_CASTS:
                    _n += 1; xcast_dv[_c] = _n
                _n += 1; sqmark[_ci] = _n
            M_VARROW = _n + 1
            M_RCASTD = _n + 2

            def _sy_xc(syn):
                for i, c in enumerate(STAGED):
                    if i >= 4:
                        pc = STAGED[i - 4]
                        if pc in SC_CASTS:
                            syn.wait_ge(s_sc, xcast_sc[pc])
                        else:
                            syn.wait_ge(s_dv, xcast_dv[pc])
                    syn.dma_start(out=xstg[:, i % 4, :],
                                  in_=xT_r[:, c, :]).then_inc(s_xc[i % 4], 16)
            block.sync(_sy_xc)

            def _sc_xc(s):
                s.wait_ge(s_dv, M_CONST)
                for c in SC_CASTS:
                    sm, val = stg_wait(c)
                    s.wait_ge(sm, val)
                    sc.inc(s.copy(x_bf[:, c, :], xstg[:, stg_slot(c), :]))
            block.scalar(_sc_xc)
            assert sc.n == xcast_sc[15]

            with nc.psum_tensor("st_ps", [P, S], F32) as st_ps, \
                 nc.psum_tensor("sq_ps", [P, S], F32) as sq_ps:

                assert DV0 == dv.n
                m2_scr = xstg[0:1, 0, :]

                def _te_p0(t):
                    t.wait_ge(s_dv, M_CONST)
                    for ci, c in enumerate(STAT_ORDER):
                        if c == 0:
                            t.wait_ge(s_x[0], 16)
                        elif c == 4:
                            t.wait_ge(s_x[2], 16)
                        elif c in SC_CASTS:
                            t.wait_ge(s_sc, xcast_sc[c])
                        elif c in DV_CASTS:
                            t.wait_ge(s_dv, xcast_dv[c])
                        for tb in range(4):
                            t.matmul(st_ps[0:1, tb * SB:(tb + 1) * SB], ones_c[:],
                                     x_bf[:, c, tb * SB:(tb + 1) * SB],
                                     start=(ci == 0), stop=(ci == C - 1))
                        t.wait_ge(s_dv, sqmark[ci])
                        ii = None
                        for tb in range(4):
                            ii = t.matmul(sq_ps[0:1, tb * SB:(tb + 1) * SB], ones_c[:],
                                          sq_scr[:, ci % 2, tb * SB:(tb + 1) * SB],
                                          start=(ci == 0), stop=(ci == C - 1))
                        marks["stat"][ci] = te.inc(ii)
                block.tensor(_te_p0)

                def _dv_p0a(v):
                    for ci, c in enumerate(STAT_ORDER):
                        if c in DV_CASTS:
                            sm, val = stg_wait(c)
                            v.wait_ge(sm, val)
                            dv.inc(v.tensor_copy(x_bf[:, c, :],
                                                 xstg[:, stg_slot(c), :]))
                        if c in (0, 1):
                            v.wait_ge(s_x[0], 16)
                        elif c == 4:
                            v.wait_ge(s_x[2], 16)
                        elif c in SC_CASTS:
                            v.wait_ge(s_sc, xcast_sc[c])
                        if ci >= 2:
                            v.wait_ge(s_te, marks["stat"][ci - 2])
                        dv.inc(v.tensor_mul(sq_scr[:, ci % 2, :], x_bf[:, c, :], x_bf[:, c, :]))
                        assert dv.n == sqmark[ci]
                    v.wait_ge(s_sc, M_MROW)
                    v.tensor_mul(m2_scr, m_row[:], m_row[:])
                    dv.inc(v.scalar_tensor_tensor(r_row[:], sq_ps[0:1, :],
                                                  cst[0:1, 1:2], m2_scr,
                                                  ALU.mult, ALU.subtract))
                    assert dv.n == M_VARROW
                    v.wait_ge(s_sc, M_RSTD1)
                    dv.inc(v.tensor_copy(r_rowb[:], r_row[:]))
                    assert dv.n == M_RCASTD
                block.vector(_dv_p0a)

                def _sc_rows(s):
                    s.wait_ge(s_te, marks["stat"][C - 1])
                    sc.inc(s.activation(m_row[:], st_ps[0:1, :], AF.Copy,
                                        scale=cst[0:1, 1:2]))
                    sc.inc(s.copy(m_rowb[:], m_row[:]))
                    s.wait_ge(s_dv, M_VARROW)
                    s.activation(r_row[:], r_row[:], AF.Ln, bias=cst[0:1, 0:1])
                    sc.inc(s.activation(r_row[:], r_row[:], AF.Exp, scale=cst[0:1, 3:4]))
                block.scalar(_sc_rows)
                assert sc.n == M_RSTD1

            with nc.psum_tensor("bcm_ps", [P, S], F32) as bcm_ps, \
                 nc.psum_tensor("bcr_ps", [P, S], F32) as bcr_ps:

                def _te_p0b(t):
                    t.wait_ge(s_sc, M_MROWB)
                    ii = None
                    for tb in range(4):
                        ii = t.matmul(bcm_ps[:, tb * SB:(tb + 1) * SB], ones_rb[:],
                                      m_rowb[0:1, tb * SB:(tb + 1) * SB], start=True, stop=True)
                    marks["bcm"] = te.inc(ii)
                    t.wait_ge(s_dv, M_RCASTD)
                    for tb in range(4):
                        ii = t.matmul(bcr_ps[:, tb * SB:(tb + 1) * SB], ones_rb[:],
                                      r_rowb[0:1, tb * SB:(tb + 1) * SB], start=True, stop=True)
                    marks["bcr"] = te.inc(ii)
                block.tensor(_te_p0b)

                def _sc_p0b(s):
                    s.wait_ge(s_te, marks["bcm"])
                    sc.inc(s.copy(m_bc[:], bcm_ps[:]))
                    s.wait_ge(s_te, marks["bcr"])
                    sc.inc(s.copy(r_bc[:], bcr_ps[:]))
                block.scalar(_sc_p0b)
                assert sc.n == M_BCEVR

                DVA = dv.n

                def aph0_mark(c):
                    return DVA + c + 1

                def aph1_mark(c):
                    return DVA + C + c + 1

                def _dv_p0c(v):
                    v.wait_ge(s_sc, M_BCEVM)
                    for c in range(4):
                        v.tensor_sub(x_bf[:, c, 0:SO], x_bf[:, c, 0:SO], m_bc[:, 0:SO])
                    v.wait_ge(s_sc, M_BCEVR)
                    for c in range(C):
                        if c >= 4:
                            v.tensor_sub(x_bf[:, c, 0:SO], x_bf[:, c, 0:SO], m_bc[:, 0:SO])
                        dv.inc(v.tensor_mul(x_bf[:, c, 0:SO], x_bf[:, c, 0:SO], r_bc[:, 0:SO]))
                    for c in range(C):
                        v.tensor_sub(x_bf[:, c, SO:S], x_bf[:, c, SO:S], m_bc[:, SO:S])
                        dv.inc(v.tensor_mul(x_bf[:, c, SO:S], x_bf[:, c, SO:S], r_bc[:, SO:S]))
                block.vector(_dv_p0c)
                assert dv.n == aph1_mark(C - 1)

            with nc.psum_tensor("p1_ps", [P, 4, SB], F32) as p1_ps, \
                 nc.psum_tensor("v_ps", [P, NT, HD], F32) as v_ps:

                def _te_p1(t):
                    def k_half(hh):
                        if hh == 0:
                            t.wait_ge(s_wk, 16)
                        ii = None
                        for c in range(C):
                            t.wait_ge(s_dv, aph0_mark(c) if hh == 0 else aph1_mark(c))
                            for tb in range(2):
                                ii = t.matmul(p1_ps[:, 2 * hh + tb, :], wk_s[:, c, :],
                                              x_bf[:, c, hh * SO + tb * SB:
                                                   hh * SO + (tb + 1) * SB],
                                              start=(c == 0), stop=(c == C - 1))
                        marks["kA" if hh == 0 else "kB"] = te.inc(ii)

                    def v_group(g):
                        if g == 0:
                            t.wait_ge(s_wv, 16)
                        ii = None
                        for tt in range(4 * g, 4 * g + 4):
                            for c in range(C):
                                ii = t.matmul(v_ps[:, tt, :], x_bf[:, c, tt * P:(tt + 1) * P],
                                              wv_s[:, c, :], start=(c == 0), stop=(c == C - 1))
                        marks["v"][g] = te.inc(ii)

                    def q_tile(f):
                        q, fi = f // 4, f % 4
                        if f in (0, 4, 8):
                            t.wait_ge(*wgate(f"wq{q}"))
                        if f == 0:
                            t.wait_ge(s_sc, sc_ev["kA"])
                        elif f == 1:
                            t.wait_ge(s_sc, sc_ev["kB"])
                        else:
                            t.wait_ge(s_sc, sc_ev[f"q{f - 2}"])
                        ii = None
                        for c in range(C):
                            for sb in range(NSB):
                                ii = t.matmul(p1_ps[:, 2 * (f % 2) + sb, :],
                                              wq_q[:, q % 2, c, fi * P:(fi + 1) * P],
                                              x_bf[:, c, sb * SB:(sb + 1) * SB],
                                              start=(c == 0), stop=(c == C - 1))
                        marks["q"][f] = te.inc(ii)

                    k_half(0)
                    v_group(0)
                    q_tile(0)
                    v_group(1)
                    q_tile(2)
                    k_half(1)
                    q_tile(1)
                    q_tile(3)
                    v_group(2)
                    q_tile(4)
                    q_tile(5)
                    v_group(3)
                    for f in (6, 7, 8, 9, 10, 11):
                        q_tile(f)
                    marks["p1_done"] = te.n
                block.tensor(_te_p1)

                def _sc_p1(s):
                    for name in EVAC_ORDER:
                        if name in ("kA", "kB"):
                            hh = 0 if name == "kA" else 1
                            s.wait_ge(s_te, marks[name])
                            ii = None
                            for tb in range(2):
                                ii = s.copy(K_sb[:, (2 * hh + tb) * SB:
                                                 (2 * hh + tb + 1) * SB],
                                            p1_ps[:, 2 * hh + tb, :])
                            sc.inc(ii)
                        elif name[0] == "v":
                            g = int(name[1:])
                            s.wait_ge(s_te, marks["v"][g])
                            sc.inc(s.copy(V_sb[:, 4 * g:4 * g + 4, :],
                                          v_ps[:, 4 * g:4 * g + 4, :]))
                        else:
                            f = int(name[1:])
                            s.wait_ge(s_te, marks["q"][f])
                            ii = None
                            for sb in range(NSB):
                                ii = s.copy(Q_sb[:, f, sb * SB:(sb + 1) * SB],
                                            p1_ps[:, 2 * (f % 2) + sb, :])
                            sc.inc(ii)
                        assert sc.n == sc_ev[name]
                block.scalar(_sc_p1)
                assert sc.n == SC_P1_END

            def _gp_a(g):
                g.dma_start(out=x_bf[:, 0:2, :],
                            in_=xT_r[:, 0:2, :]).then_inc(s_x[0], 16)
                g.dma_start(out=x_bf[:, 4:5, :],
                            in_=xT_r[:, 4:5, :]).then_inc(s_x[2], 16)
                g.dma_start(out=wk_s[:], in_=wk_r[:]).then_inc(s_wk, 16)
                g.dma_start(out=wv_s[:], in_=wv_r[:]).then_inc(s_wv, 16)
                for q in range(3):
                    if q >= 2:
                        g.wait_ge(s_te, marks["q"][4 * (q - 2) + 3])
                    g.dma_start(out=wq_q[:, q % 2],
                                in_=wq_r[:, :, q * SB:(q + 1) * SB]).then_inc(wgate(f"wq{q}")[0], 16)
            block.gpsimd(_gp_a)

            if debug:
                def _sy_dbg1(syn):
                    syn.wait_ge(s_dv, M_VARROW)
                    syn.wait_ge(s_sc, M_RSTD1)
                    syn.dma_start(out=dbg["m"], in_=m_row[:]).then_inc(s_y, 16)
                    syn.dma_start(out=dbg["r"], in_=r_row[:]).then_inc(s_y, 16)
                    syn.wait_ge(s_sc, SC_P1_END)
                    syn.dma_start(out=dbg["k"], in_=K_sb[:]).then_inc(s_y, 16)
                    syn.dma_start(out=dbg["v"], in_=V_sb[:]).then_inc(s_y, 16)
                    syn.dma_start(out=dbg["q"], in_=Q_sb[:]).then_inc(s_y, 16)
                block.sync(_sy_dbg1)

        if True:
            x_res = xbuf[:].bitcast(F32)
            m2_row = None
            r2_row = None

            with nc.sbuf_tensor("o_sb", [P, H, SO], BF16) as O_sb, \
                 nc.sbuf_tensor("wo_q", [P, 2, C, 256], BF16) as wo_q:

                wq3v = wo_q[:].rearrange("p a c f -> p (a c f)").rearrange(
                    "p (c f) -> p c f", f=SB)

                def _gp_wq3(g):
                    g.wait_ge(s_te, marks["p1_done"])
                    g.dma_start(out=wq3v,
                                in_=wq_r[:, :, 3 * SB:4 * SB]).then_inc(s_wq3, 16)
                block.gpsimd(_gp_wq3)

                with nc.sbuf_tensor("e_sb", [P, 2, NT * SB], BF16) as E, \
                     nc.sbuf_tensor("tsc", [P, 2, NT * SB // 2], BF16) as TSC, \
                     nc.psum_tensor("qk_ps", [P, 4, SB], F32) as qk_ps, \
                     nc.psum_tensor("pv_ps", [P, 2, SB], F32) as pv_ps, \
                     nc.psum_tensor("d_ps", [P, SB], F32) as d_ps, \
                     nc.psum_tensor("wo_ps", [P, SB], F32) as wo_ps:

                    SC2 = sc.n
                    DV2 = dv.n

                    def xcast2_mark(c):
                        return SC_P2_END + c + 1

                    expb_mark = {}
                    ln_mark = {}
                    expd_mark = {}
                    _n = SC2
                    for _i in range(NITER):
                        for _j in range(2):
                            _n += 1; expb_mark[8 * _i + _j] = _n
                        if _i >= 2:
                            _n += 1; ln_mark[_i - 2] = _n
                        for _j in (2, 3, 4, 5):
                            _n += 1; expb_mark[8 * _i + _j] = _n
                        if _i >= 2:
                            _n += 1; expd_mark[_i - 2] = _n
                        for _j in (6, 7):
                            _n += 1; expb_mark[8 * _i + _j] = _n
                    _n += 1; ln_mark[30] = _n
                    _n += 1; expd_mark[30] = _n
                    _n += 1; ln_mark[31] = _n
                    _n += 1; expd_mark[31] = _n
                    SC_P2_END = _n

                    tree_mark = {}
                    dsp_mark = {}
                    onorm_mark = {}
                    woep_mark = {}
                    qev2_mark = {}
                    _n = DV2
                    for _i in range(NITER):
                        if 2 <= _i <= 16 and _i % 2 == 0:
                            _n += 1; qev2_mark[(_i - 2) // 2] = _n
                        if _i >= 19:
                            _n += 1; woep_mark[_i - 19] = _n
                        if _i >= 2:
                            _n += 1; dsp_mark[_i - 2] = _n
                        if _i >= 1:
                            _n += 1; tree_mark[_i - 1] = _n
                        if _i >= 2:
                            _n += 1; onorm_mark[_i - 2] = _n
                    sq2_mark = {}
                    _n += 1; woep_mark[13] = _n
                    _n += 1; tree_mark[31] = _n
                    _n += 1; dsp_mark[30] = _n
                    _n += 1; onorm_mark[30] = _n
                    _n += 1; woep_mark[14] = _n
                    _n += 1; dsp_mark[31] = _n
                    _n += 1; onorm_mark[31] = _n
                    for _k in range(15, 32):
                        _n += 1; woep_mark[_k] = _n
                        if _k >= 16:
                            _n += 1; sq2_mark[_k - 16] = _n
                    DV_P2_END = _n

                    def wo_gp_gate(k):
                        rnd, p = k // 16, (k % 16) // 2
                        return wgate(f"wo{p}" if rnd == 0 else f"wob{p}")

                    def wo_lhsT(k, h):
                        p = (k % 16) // 2
                        f = k % 16
                        return wo_q[:, p % 2, h, (f % 2) * P:(f % 2 + 1) * P]

                    def _te_wo_group(t, k, h0, h1):
                        sbw = k // 16
                        out_ap = wo_ps[:] if k <= 15 else qk_ps[:, k % 4, :]
                        ii = None
                        for h in range(h0, h1):
                            ii = t.matmul(out_ap, wo_lhsT(k, h),
                                          O_sb[:, h, sbw * SB:(sbw + 1) * SB],
                                          start=(h == 0), stop=(h == H - 1))
                        return ii

                    def _te_p2(t):
                        t.wait_ge(s_dv, DV2)
                        t.wait_ge(s_sc, SC_P1_END)
                        for i in range(NITER):
                            sb, h = i // 16, i % 16
                            pj = (i - 1) % 2
                            qsl = Q_sb[:, h, sb * SB:(sb + 1) * SB]

                            def qk_batch(j, i=i, qsl=qsl):
                                g = 8 * i + j
                                if g >= 2:
                                    t.wait_ge(s_sc, expb_mark[g - 2])
                                ii = None
                                for u in range(2):
                                    c = 2 * j + u
                                    ii = t.matmul(qk_ps[:, c % 4, :],
                                                  K_sb[:, c * P:(c + 1) * P], qsl,
                                                  start=True, stop=True)
                                marks["qk"][g] = te.inc(ii)

                            def pv_part(c0, c1, i=i, pj=pj):
                                ii = None
                                for c in range(c0, c1):
                                    ii = t.matmul(pv_ps[:, pj, :], V_sb[:, c, :],
                                                  E[:, pj, c * SB:(c + 1) * SB],
                                                  start=(c == 0), stop=(c == NT - 1))
                                if c1 == NT:
                                    marks["pv"][i - 1] = te.inc(ii)

                            if i >= 2:
                                t.wait_ge(s_dv, tree_mark[i - 2])
                                if i >= 3:
                                    t.wait_ge(s_sc, expd_mark[i - 3])
                                marks["dones"][i - 2] = te.inc(
                                    t.matmul(d_ps[0:1, :], ones_c[:],
                                             TSC[:, (i - 2) % 2, 0:SB], start=True, stop=True))
                            if h >= 12:
                                t.wait_ge(s_dv, qev2_mark[2 * (h - 12)])
                                t.wait_ge(s_dv, qev2_mark[2 * (h - 12) + sb])
                            qk_batch(0)
                            qk_batch(1)
                            if i >= 1:
                                t.wait_ge(s_sc, expb_mark[8 * (i - 1) + 7])
                                if i >= 3:
                                    t.wait_ge(s_dv, onorm_mark[i - 3])
                                pv_part(0, 4)
                            qk_batch(2)
                            if i >= 1:
                                pv_part(4, 8)
                            qk_batch(3)
                            if i >= 1:
                                pv_part(8, 12)
                            qk_batch(4)
                            if i >= 1:
                                pv_part(12, 16)
                            qk_batch(5)
                            if i >= 2:
                                t.wait_ge(s_dv, dsp_mark[i - 2])
                                t.matmul(d_ps[:], ones_rb[:], drow_hi[:],
                                         start=True, stop=False)
                                marks["dbc"][i - 2] = te.inc(
                                    t.matmul(d_ps[:], ones_rb[:], drow_lo[:],
                                             start=False, stop=True))
                            qk_batch(6)
                            if i >= 18:
                                k = i - 18
                                if k == 0:
                                    t.wait_ge(s_dv, onorm_mark[15])
                                else:
                                    t.wait_ge(s_dv, woep_mark[k - 1])
                                if k % 2 == 0:
                                    t.wait_ge(*wo_gp_gate(k))
                                _te_wo_group(t, k, 0, 8)
                            qk_batch(7)
                            if i >= 18:
                                marks["wo"][i - 18] = te.inc(_te_wo_group(t, i - 18, 8, H))
                            if i <= 15:
                                g2, hh = i // 2, i % 2
                                f2, sb2 = 12 + g2 // 2, g2 % 2
                                if i == 0:
                                    t.wait_ge(s_wq3, 16)
                                if hh == 0 and g2 >= 1:
                                    t.wait_ge(s_dv, qev2_mark[g2 - 1])
                                ii = None
                                for c in range(8 * hh, 8 * hh + 8):
                                    ii = t.matmul(wo_ps[:],
                                                  wq3v[:, c, (f2 - 12) * P:(f2 - 11) * P],
                                                  x_bf[:, c, sb2 * SB:(sb2 + 1) * SB],
                                                  start=(c == 0), stop=(c == C - 1))
                                if hh == 1:
                                    marks["q2"][g2] = te.inc(ii)
                        t.wait_ge(s_sc, expb_mark[8 * NITER - 1])
                        t.wait_ge(s_dv, onorm_mark[NITER - 3])
                        pj = (NITER - 1) % 2
                        ii = None
                        for c in range(NT):
                            ii = t.matmul(pv_ps[:, pj, :], V_sb[:, c, :],
                                          E[:, pj, c * SB:(c + 1) * SB],
                                          start=(c == 0), stop=(c == NT - 1))
                        marks["pv"][NITER - 1] = te.inc(ii)

                        def tail_d(j2):
                            t.wait_ge(s_dv, tree_mark[j2])
                            t.wait_ge(s_sc, expd_mark[j2 - 1])
                            marks["dones"][j2] = te.inc(
                                t.matmul(d_ps[0:1, :], ones_c[:], TSC[:, j2 % 2, 0:SB],
                                         start=True, stop=True))

                        def tail_dbc(j2):
                            t.wait_ge(s_dv, dsp_mark[j2])
                            t.matmul(d_ps[:], ones_rb[:], drow_hi[:],
                                     start=True, stop=False)
                            marks["dbc"][j2] = te.inc(
                                t.matmul(d_ps[:], ones_rb[:], drow_lo[:],
                                         start=False, stop=True))

                        def tail_wo(k):
                            if k == 14:
                                t.wait_ge(s_dv, woep_mark[13])
                            elif k == 15:
                                t.wait_ge(s_dv, woep_mark[14])
                            else:
                                if k == 16:
                                    t.wait_ge(s_dv, onorm_mark[NITER - 1])
                                if k >= 20:
                                    t.wait_ge(s_dv, woep_mark[k - 4])
                            if k % 2 == 0:
                                t.wait_ge(*wo_gp_gate(k))
                            marks["wo"][k] = te.inc(_te_wo_group(t, k, 0, H))

                        def stats2_c(c):
                            t.wait_ge(s_dv, sq2_mark[c])
                            for tb in range(2):
                                t.matmul(pv_ps[0:1, tb, :], ones_c[:],
                                         xb2[:, c, tb * SB:(tb + 1) * SB],
                                         start=(c == 0), stop=(c == C - 1))
                            ii = None
                            for tb in range(2):
                                dst = d_ps[0:1, :] if tb == 0 else wo_ps[0:1, :]
                                ii = t.matmul(dst, ones_c[:],
                                              v_flat[:, (c % 2) * SO + tb * SB:
                                                     (c % 2) * SO + (tb + 1) * SB],
                                              start=(c == 0), stop=(c == C - 1))
                            marks["stat2"][c] = te.inc(ii)

                        tail_d(30)
                        tail_wo(14)
                        tail_dbc(30)
                        tail_d(31)
                        tail_wo(15)
                        tail_dbc(31)
                        for k in range(16, 32):
                            tail_wo(k)
                            if k == 18:
                                t.wait_ge(s_dv, onorm_mark[NITER - 1])
                                t.wait_ge(s_sc, expd_mark[NITER - 1])
                            if k >= 18:
                                stats2_c(k - 18)
                        stats2_c(14)
                        stats2_c(15)
                    block.tensor(_te_p2)

                    def _sy_xres(syn):
                        syn.wait_ge(s_te, marks["q2"][7])
                        syn.dma_start(out=x_res[:],
                                      in_=xT_r[:, :, 0:SO]).then_inc(s_xres, 16)
                    block.sync(_sy_xres)

                    def _sc_p2(s):
                        def d_ln(jj):
                            s.wait_ge(s_te, marks["dones"][jj])
                            if jj >= 1:
                                s.wait_ge(s_dv, dsp_mark[jj - 1])
                            sc.inc(s.activation(drow[0:1, :], d_ps[0:1, :], AF.Ln))
                            assert sc.n == ln_mark[jj]

                        def d_exp(jj):
                            s.wait_ge(s_te, marks["dbc"][jj])
                            if jj >= 1:
                                s.wait_ge(s_dv, onorm_mark[jj - 1])
                            sc.inc(s.activation(dbc_sb[:], d_ps[:], AF.Exp,
                                                scale=cst[:, 4:5]))
                            assert sc.n == expd_mark[jj]

                        for i in range(NITER):
                            pi = i % 2
                            if i >= 2:
                                s.wait_ge(s_te, marks["pv"][i - 2])
                                s.wait_ge(s_dv, tree_mark[i - 2])

                            def batch(j, i=i, pi=pi):
                                g = 8 * i + j
                                s.wait_ge(s_te, marks["qk"][g])
                                r = j % 2
                                sc.inc(s.activation(E[:, pi, (2 * j) * SB:(2 * j + 2) * SB],
                                                    qk_ps[:, 2 * r:2 * r + 2, :],
                                                    AF.Exp, scale=cst[:, 2:3]))
                                assert sc.n == expb_mark[g]

                            batch(0)
                            batch(1)
                            if i >= 2:
                                d_ln(i - 2)
                            for j in (2, 3, 4, 5):
                                batch(j)
                            if i >= 2:
                                d_exp(i - 2)
                            batch(6)
                            batch(7)
                        d_ln(30)
                        d_exp(30)
                        d_ln(31)
                        d_exp(31)
                        assert sc.n == SC_P2_END
                    block.scalar(_sc_p2)

                    def _po_p2(g):
                        for p in range(2):
                            g.wait_ge(s_te, marks["q2"][7])
                            g.dma_start(out=wo_q[:, p % 2],
                                        in_=wo_r[:, :, p * 256:(p + 1) * 256]).then_inc(
                                            wgate(f"wo{p}")[0], 16)
                        for p in range(2, 8):
                            g.wait_ge(s_te, marks["wo"][2 * (p - 2) + 1])
                            g.dma_start(out=wo_q[:, p % 2],
                                        in_=wo_r[:, :, p * 256:(p + 1) * 256]).then_inc(
                                            wgate(f"wo{p}")[0], 16)
                    block.gpsimd(_po_p2)

                    def _dv_p2(v):
                        half = NT * SB // 2

                        def wo_ep(k):
                            sbw, f = k // 16, k % 16
                            if k == 0:
                                v.wait_ge(s_xres, 16)
                            v.wait_ge(s_te, marks["wo"][k])
                            src_ap = wo_ps[:] if k <= 15 else qk_ps[:, k % 4, :]
                            dv.inc(v.tensor_add(x_res[:, f, sbw * SB:(sbw + 1) * SB],
                                                x_res[:, f, sbw * SB:(sbw + 1) * SB],
                                                src_ap))
                            assert dv.n == woep_mark[k]

                        def tree(jj):
                            pk = jj % 2
                            v.wait_ge(s_sc, expb_mark[8 * jj + 7])
                            if jj >= 2:
                                v.wait_ge(s_te, marks["dones"][jj - 2])
                            v.tensor_add(TSC[:, pk, :], E[:, pk, 0:half],
                                         E[:, pk, half:2 * half])
                            v.tensor_add(TSC[:, pk, 0:half // 2], TSC[:, pk, 0:half // 2],
                                         TSC[:, pk, half // 2:half])
                            v.tensor_add(TSC[:, pk, 0:half // 4], TSC[:, pk, 0:half // 4],
                                         TSC[:, pk, half // 4:half // 2])
                            dv.inc(v.tensor_add(TSC[:, pk, 0:half // 8], TSC[:, pk, 0:half // 8],
                                                TSC[:, pk, half // 8:half // 4]))
                            assert dv.n == tree_mark[jj]

                        def dsplit(jj):
                            v.wait_ge(s_sc, ln_mark[jj])
                            v.tensor_copy(drow_hi[:], drow[0:1, :])
                            dv.inc(v.tensor_sub(drow_lo[:], drow[0:1, :], drow_hi[:]))
                            assert dv.n == dsp_mark[jj]

                        def onorm(jj):
                            pk = jj % 2
                            sbp, hp = jj // 16, jj % 16
                            v.wait_ge(s_sc, expd_mark[jj])
                            dv.inc(v.tensor_mul(O_sb[:, hp, sbp * SB:(sbp + 1) * SB],
                                                pv_ps[:, pk, :], dbc_sb[:]))
                            assert dv.n == onorm_mark[jj]

                        for i in range(NITER):
                            if 2 <= i <= 16 and i % 2 == 0:
                                g2 = (i - 2) // 2
                                f2, sb2 = 12 + g2 // 2, g2 % 2
                                v.wait_ge(s_te, marks["q2"][g2])
                                dv.inc(v.tensor_copy(
                                    Q_sb[:, f2, sb2 * SB:(sb2 + 1) * SB], wo_ps[:]))
                                assert dv.n == qev2_mark[g2]
                            if i >= 19:
                                wo_ep(i - 19)
                            if i >= 2:
                                dsplit(i - 2)
                            if i >= 1:
                                tree(i - 1)
                            if i >= 2:
                                onorm(i - 2)
                        def sq2(c):
                            v.wait_ge(s_sc, xcast2_mark(c))
                            if c >= 2:
                                v.wait_ge(s_te, marks["stat2"][c - 2])
                            dv.inc(v.tensor_mul(v_flat[:, (c % 2) * SO:(c % 2 + 1) * SO],
                                                xb2[:, c, :], xb2[:, c, :]))
                            assert dv.n == sq2_mark[c]

                        wo_ep(13)
                        tree(31)
                        dsplit(30)
                        onorm(30)
                        wo_ep(14)
                        dsplit(31)
                        onorm(31)
                        for k in range(15, 32):
                            wo_ep(k)
                            if k >= 16:
                                sq2(k - 16)
                        assert dv.n == DV_P2_END
                    block.vector(_dv_p2)

                    def _gp_b(g):
                        for p in range(8):
                            if p < 2:
                                g.wait_ge(s_te, marks["wo"][13 if p == 0 else 15])
                            else:
                                g.wait_ge(s_te, marks["wo"][16 + 2 * (p - 2) + 1])
                            g.dma_start(out=wo_q[:, p % 2],
                                        in_=wo_r[:, :, p * 256:(p + 1) * 256]).then_inc(
                                            wgate(f"wob{p}")[0], 16)
                    block.gpsimd(_gp_b)

                    def _sc_p4cast(s):
                        for c in range(C):
                            s.wait_ge(s_dv, woep_mark[16 + c])
                            sc.inc(s.copy(xb2[:, c, :], x_res[:, c, :]))
                    block.scalar(_sc_p4cast)
                    assert sc.n == xcast2_mark(C - 1)

                    pv01 = pv_ps[0:1, :, :].rearrange("p a b -> p (a b)")
                    _tscf = TSC[:].bitcast(F32).rearrange("p a b -> p (a b)")
                    _tscb = TSC[:].rearrange("p a b -> p (a b)")
                    m2_row = _tscf[0:1, 0:SO]
                    r2_row = _tscf[0:1, SO:2 * SO]
                    m2_rowb = _tscb[0:1, 4 * SO:5 * SO]
                    r2_rowb = _tscb[0:1, 5 * SO:6 * SO]

                    m2sq_row = _tscf[0:1, 3 * SO:4 * SO]

                    def _dv_m2(v):
                        v.wait_ge(s_te, marks["stat2"][C - 1])
                        v.tensor_scalar_mul(m2_row, pv01, cst[0:1, 1:2])
                        v.tensor_copy(m2_rowb, m2_row)
                        v.tensor_mul(m2sq_row, m2_row, m2_row)
                        v.scalar_tensor_tensor(_tscf[0:1, SO:SO + SB], d_ps[0:1, :],
                                               cst[0:1, 1:2],
                                               _tscf[0:1, 3 * SO:3 * SO + SB],
                                               ALU.mult, ALU.subtract)
                        dv.inc(v.scalar_tensor_tensor(
                            _tscf[0:1, SO + SB:2 * SO], wo_ps[0:1, :],
                            cst[0:1, 1:2], _tscf[0:1, 3 * SO + SB:4 * SO],
                            ALU.mult, ALU.subtract))
                    block.vector(_dv_m2)
                    M_VARROW2 = dv.n

                    def _sc_m2(s):
                        s.wait_ge(s_dv, M_VARROW2)
                        s.activation(r2_row, r2_row, AF.Ln, bias=cst[0:1, 0:1])
                        sc.inc(s.activation(r2_row, r2_row, AF.Exp,
                                            scale=cst[0:1, 3:4]))
                    block.scalar(_sc_m2)
                    M_RSTD2 = sc.n

                    def _dv_r2cast(v):
                        v.wait_ge(s_sc, M_RSTD2)
                        dv.inc(v.tensor_copy(r2_rowb, r2_row))
                    block.vector(_dv_r2cast)
                    M_R2CAST = dv.n

                if debug:
                    def _sy_dbg2(syn):
                        syn.wait_ge(s_dv, onorm_mark[NITER - 1])
                        syn.dma_start(out=dbg["o"], in_=O_sb[:]).then_inc(s_y, 16)
                        syn.wait_ge(s_dv, DV_P2_END)
                        syn.dma_start(out=dbg["x1"], in_=x_res[:]).then_inc(s_y, 16)
                    block.sync(_sy_dbg2)

            with nc.sbuf_tensor("wt_q", [P, 2, C, 256], BF16) as wt_q, \
                 nc.sbuf_tensor("m2_bc", [P, SO], BF16) as m2_bc, \
                 nc.sbuf_tensor("r2_bc", [P, SO], BF16) as r2_bc, \
                 nc.sbuf_tensor("sw_row", [1, D], F32) as sw_row, \
                 nc.sbuf_tensor("bias_row", [1, D], F32) as bias_row, \
                 nc.sbuf_tensor("sw_col", [P, C], F32) as sw_col, \
                 nc.sbuf_tensor("bias_col", [P, C], F32) as bias_col, \
                 nc.sbuf_tensor("t1_scr", [P, 2, SB], F32) as t1_scr:

                DV4 = dv.n

                with nc.psum_tensor("bc2m_ps", [P, SO], F32) as bc2m_ps, \
                     nc.psum_tensor("bc2r_ps", [P, SO], F32) as bc2r_ps, \
                     nc.psum_tensor("swc_ps", [P, C], F32) as swc_ps, \
                     nc.psum_tensor("bic_ps", [P, C], F32) as bic_ps:

                    def _te_p4b(t):
                        t.wait_ge(s_dv, M_R2CAST)
                        ii = None
                        for tb in range(2):
                            t.matmul(bc2m_ps[:, tb * SB:(tb + 1) * SB], ones_rb[:],
                                     m2_rowb[:, tb * SB:(tb + 1) * SB], start=True, stop=True)
                            ii = t.matmul(bc2r_ps[:, tb * SB:(tb + 1) * SB], ones_rb[:],
                                          r2_rowb[:, tb * SB:(tb + 1) * SB], start=True, stop=True)
                        marks["bc2"] = te.inc(ii)
                        t.wait_ge(s_swr, 16)
                        t.wait_ge(s_bir, 16)
                        for c in range(C):
                            t.matmul(swc_ps[:, c:c + 1], sw_row[0:1, c * P:(c + 1) * P],
                                     ones_r[0:1, 0:1], start=(c == 0), stop=(c == C - 1))
                        ii = None
                        for c in range(C):
                            ii = t.matmul(bic_ps[:, c:c + 1], bias_row[0:1, c * P:(c + 1) * P],
                                          ones_r[0:1, 0:1], start=(c == 0), stop=(c == C - 1))
                        marks["cols"] = te.inc(ii)
                    block.tensor(_te_p4b)

                    def _sc_p4ev(s):
                        s.wait_ge(s_te, marks["bc2"])
                        sc.inc(s.copy(m2_bc[:], bc2m_ps[:]))
                        sc.inc(s.copy(r2_bc[:], bc2r_ps[:]))
                    block.scalar(_sc_p4ev)
                    M_BC2EV = sc.n

                    DVB = dv.n

                    def apply2_mark(c):
                        return DVB + 1 + c + 1

                    def _dv_p4c(v):
                        v.wait_ge(s_te, marks["cols"])
                        v.tensor_copy(sw_col[:], swc_ps[:])
                        dv.inc(v.tensor_copy(bias_col[:], bic_ps[:]))
                        v.wait_ge(s_sc, M_BC2EV)
                        for c in range(C):
                            v.tensor_sub(xb2[:, c, :], xb2[:, c, :], m2_bc[:])
                            dv.inc(v.tensor_mul(xb2[:, c, :], xb2[:, c, :], r2_bc[:]))
                    block.vector(_dv_p4c)
                    M_COLS = dv.n
                    assert M_COLS == apply2_mark(C - 1)

                with nc.psum_tensor("ffn_ps", [P, 4, SB], F32) as ffn_ps:
                    DVF = dv.n

                    def ffep_mark(k):
                        return DVF + k + 1

                    def _te_p4c(t):
                        for g in range(8):
                            for k in range(4 * g, 4 * g + 4):
                                o, sb = k // 2, k % 2
                                if k % 4 == 0:
                                    t.wait_ge(*wgate(f"wt{g}"))
                                if k == 0:
                                    t.wait_ge(s_dv, apply2_mark(7))
                                if k >= 4:
                                    t.wait_ge(s_dv, ffep_mark(k - 4))
                                for c in range(8):
                                    t.matmul(ffn_ps[:, k % 4, :],
                                             wt_q[:, g % 2, c, (o % 2) * P:(o % 2 + 1) * P],
                                             xb2[:, c, sb * SB:(sb + 1) * SB],
                                             start=(c == 0), stop=False)
                            for k in range(4 * g, 4 * g + 4):
                                o, sb = k // 2, k % 2
                                if k == 0:
                                    t.wait_ge(s_dv, M_COLS)
                                ii = None
                                for c in range(8, C):
                                    ii = t.matmul(ffn_ps[:, k % 4, :],
                                                  wt_q[:, g % 2, c, (o % 2) * P:(o % 2 + 1) * P],
                                                  xb2[:, c, sb * SB:(sb + 1) * SB],
                                                  start=False, stop=(c == C - 1))
                                marks["ffn"][k] = te.inc(ii)
                    block.tensor(_te_p4c)

                    def _dv_p4d(v):
                        for k in range(32):
                            o, sb = k // 2, k % 2
                            v.wait_ge(s_te, marks["ffn"][k])
                            v.tensor_scalar(t1_scr[:, k % 2, :], ffn_ps[:, k % 4, :],
                                            sw_col[:, o:o + 1], bias_col[:, o:o + 1],
                                            ALU.mult, ALU.add)
                            dv.inc(v.tensor_add(x_res[:, o, sb * SB:(sb + 1) * SB],
                                                x_res[:, o, sb * SB:(sb + 1) * SB],
                                                t1_scr[:, k % 2, :]))
                    block.vector(_dv_p4d)
                    M_FFN_DONE = dv.n
                    assert M_FFN_DONE == ffep_mark(31)

                def _gp_c(g):
                    for q in range(8):
                        if q < 2:
                            g.wait_ge(s_te, marks["wo"][31])
                        else:
                            g.wait_ge(s_te, marks["ffn"][4 * (q - 2) + 3])
                        g.dma_start(out=wt_q[:, q % 2],
                                    in_=wt_r[:, :, q * 256:(q + 1) * 256]).then_inc(
                                        wgate(f"wt{q}")[0], 16)
                block.gpsimd(_gp_c)

                def _sy_b(syn):
                    syn.wait_ge(s_te, marks["wo"][31])
                    syn.dma_start(out=sw_row[:], in_=sw[:]).then_inc(s_swr, 16)
                    syn.dma_start(out=bias_row[:], in_=bias[:]).then_inc(s_bir, 16)
                    ny = 7 * 16 if debug else 0
                    for jy in range(8):
                        syn.wait_ge(s_dv, ffep_mark(4 * jy + 3))
                        syn.dma_start(out=y_r[:, 2 * jy:2 * jy + 2, :],
                                      in_=x_res[:, 2 * jy:2 * jy + 2, :]).then_inc(s_y, 16)
                        ny += 16
                    syn.wait_ge(s_y, ny)
                block.sync(_sy_b)

    return nc


_NC_CACHE = {}


def kernel(L0_attn_Q_input, L0_attn_Wq, L0_attn_Wk, L0_attn_Wv, L0_attn_Wo,
           L0_ffn_weight, L0_ffn_scale_weight, L0_ffn_bias):
    x = np.asarray(L0_attn_Q_input, dtype=np.float32)
    wq_h = np.ascontiguousarray(np.asarray(L0_attn_Wq, dtype=np.float32))
    wk_h = np.ascontiguousarray(np.asarray(L0_attn_Wk, dtype=np.float32))
    wv_h = np.ascontiguousarray(np.asarray(L0_attn_Wv, dtype=np.float32))
    wo_h = np.ascontiguousarray(np.asarray(L0_attn_Wo, dtype=np.float32))
    wt_h = np.ascontiguousarray(np.asarray(L0_ffn_weight, dtype=np.float32).T)
    sw_h = np.ascontiguousarray(np.asarray(L0_ffn_scale_weight, dtype=np.float32).reshape(1, D))
    bi_h = np.ascontiguousarray(np.asarray(L0_ffn_bias, dtype=np.float32).reshape(1, D))

    if "nc" not in _NC_CACHE:
        _NC_CACHE["nc"] = build_nc(debug=False)
    nc = _NC_CACHE["nc"]

    in_maps = []
    for core in range(8):
        b, half = core // 2, core % 2
        xTb = x[b].T
        if half == 0:
            xp = np.ascontiguousarray(xTb)
        else:
            xp = np.ascontiguousarray(np.concatenate([xTb[:, SO:], xTb[:, :SO]], axis=1))
        in_maps.append({"xT": xp, "wq": wq_h, "wk": wk_h, "wv": wv_h,
                        "wo": wo_h, "wt": wt_h, "sw": sw_h, "bias": bi_h})

    res = run_bass_kernel_spmd(nc, in_maps, core_ids=list(range(8)))

    out = np.empty((4, S, D), dtype=np.float32)
    for core in range(8):
        b, half = core // 2, core % 2
        shard = np.asarray(res.results[core]["y"])
        out[b, half * SO:(half + 1) * SO, :] = shard.T
    return out


# version 28
# speedup vs baseline: 1.0370x; 1.0050x over previous
import math
from contextlib import ExitStack

import numpy as np
import concourse.bass as bass
import concourse.mybir as mybir
from concourse.bass_utils import run_bass_kernel_spmd

F32 = mybir.dt.float32
BF16 = mybir.dt.bfloat16
AF = mybir.ActivationFunctionType
ALU = mybir.AluOpType

P = 128
D = 2048
C = 16
S = 2048
SO = 1024
H = 16
HD = 128
SB = 512
NSB = SO // SB
NT = S // P
EPS = 1e-5
SCALE = 1.0 / math.sqrt(HD)
NITER = H * NSB


class Ctr:
    def __init__(self, sem):
        self.sem = sem
        self.n = 0

    def inc(self, instr, k=1):
        instr.then_inc(self.sem, k)
        self.n += k
        return self.n


def build_nc(debug=False):
    nc = bass.Bass("TRN2", target_bir_lowering=False)

    xT = nc.dram_tensor("xT", [D, S], F32, kind="ExternalInput").ap()
    wq = nc.dram_tensor("wq", [D, D], F32, kind="ExternalInput").ap()
    wk = nc.dram_tensor("wk", [D, HD], F32, kind="ExternalInput").ap()
    wv = nc.dram_tensor("wv", [D, HD], F32, kind="ExternalInput").ap()
    wo = nc.dram_tensor("wo", [D, D], F32, kind="ExternalInput").ap()
    wt = nc.dram_tensor("wt", [D, D], F32, kind="ExternalInput").ap()
    sw = nc.dram_tensor("sw", [1, D], F32, kind="ExternalInput").ap()
    bias = nc.dram_tensor("bias", [1, D], F32, kind="ExternalInput").ap()
    y = nc.dram_tensor("y", [D, SO], F32, kind="ExternalOutput").ap()

    dbg = {}
    if debug:
        dbg["k"] = nc.dram_tensor("dbg_k", [P, S], BF16, kind="ExternalOutput").ap()
        dbg["v"] = nc.dram_tensor("dbg_v", [P, NT, HD], BF16, kind="ExternalOutput").ap()
        dbg["q"] = nc.dram_tensor("dbg_q", [P, H, SO], BF16, kind="ExternalOutput").ap()
        dbg["m"] = nc.dram_tensor("dbg_m", [1, S], F32, kind="ExternalOutput").ap()
        dbg["r"] = nc.dram_tensor("dbg_r", [1, S], F32, kind="ExternalOutput").ap()
        dbg["o"] = nc.dram_tensor("dbg_o", [P, H, SO], BF16, kind="ExternalOutput").ap()
        dbg["x1"] = nc.dram_tensor("dbg_x1", [P, C, SO], F32, kind="ExternalOutput").ap()

    xT_r = xT.rearrange("(c p) s -> p c s", p=P)
    wq_r = wq.rearrange("(c p) f -> p c f", p=P)
    wk_r = wk.rearrange("(c p) h -> p c h", p=P)
    wv_r = wv.rearrange("(c p) h -> p c h", p=P)
    wo_r = wo.rearrange("(c p) f -> p c f", p=P)
    wt_r = wt.rearrange("(c p) f -> p c f", p=P)
    y_r = y.rearrange("(c p) s -> p c s", p=P)

    WA = ["wq0", "wq2", "wo0", "wo2", "wo4", "wo6",
          "wob0", "wob2", "wob4", "wob6", "wt0", "wt2", "wt4", "wt6"]
    WB = ["wq1", "wo1", "wo3", "wo5", "wo7",
          "wob1", "wob3", "wob5", "wob7", "wt1", "wt3", "wt5", "wt7"]

    marks = {"stat": [None] * C, "v": [None] * 4, "q": [None] * C,
             "qk": [None] * (NITER * 8), "pv": [None] * NITER,
             "dones": [None] * NITER, "dbc": [None] * NITER,
             "wo": [None] * 32, "stat2": [None] * C, "ffn": [None] * 32,
             "q2": [None] * 8}

    with ExitStack() as es:
        def sbuf(*a):
            return es.enter_context(nc.sbuf_tensor(*a))
        cst = sbuf("cst", [P, 6], F32)
        ones_c = sbuf("ones_c", [P, 1], BF16)
        ones_r = sbuf("ones_r", [1, P], F32)
        ones_rb = sbuf("ones_rb", [1, P], BF16)
        drow_hi = sbuf("drow_hi", [1, SB], BF16)
        drow_lo = sbuf("drow_lo", [1, SB], BF16)
        xbuf = sbuf("xbuf", [P, C, S], BF16)
        Q_sb = sbuf("q_sb", [P, H, SO], BF16)
        K_sb = sbuf("k_sb", [P, S], BF16)
        V_sb = sbuf("v_sb", [P, NT, HD], BF16)
        xb2 = Q_sb
        v_flat = V_sb[:].rearrange("p a b -> p (a b)")
        drow = sbuf("drow", [1, SB], F32)
        dbc_sb = sbuf("dbc_sb", [P, SB], F32)

        s_te = es.enter_context(nc.semaphore("s_te"))
        s_sc = es.enter_context(nc.semaphore("s_sc"))
        s_dv = es.enter_context(nc.semaphore("s_dv"))
        s_po = es.enter_context(nc.semaphore("s_po"))
        s_x = [es.enter_context(nc.semaphore(f"s_x{i}")) for i in range(4)]
        s_wk = es.enter_context(nc.semaphore("s_wk"))
        s_wv = es.enter_context(nc.semaphore("s_wv"))
        s_wa = es.enter_context(nc.semaphore("s_wa"))
        s_wb = es.enter_context(nc.semaphore("s_wb"))
        s_xc = [es.enter_context(nc.semaphore(f"s_xc{i}")) for i in range(4)]
        s_xres = es.enter_context(nc.semaphore("s_xres"))
        s_swr = es.enter_context(nc.semaphore("s_swr"))
        s_bir = es.enter_context(nc.semaphore("s_bir"))
        s_y = es.enter_context(nc.semaphore("s_y"))
        s_wq3 = es.enter_context(nc.semaphore("s_wq3"))

        def wgate(name):
            if name in WA:
                return s_wa, 16 * (WA.index(name) + 1)
            return s_wb, 16 * (WB.index(name) + 1)

        block = es.enter_context(nc.Block())

        te, sc, dv, po = Ctr(s_te), Ctr(s_sc), Ctr(s_dv), Ctr(s_po)
        n_y = 0

        def _consts(v):
            v.memset(cst[:, 0:1], EPS)
            v.memset(cst[:, 1:2], 1.0 / D)
            v.memset(cst[:, 2:3], SCALE)
            v.memset(cst[:, 3:4], -0.5)
            v.memset(cst[:, 4:5], -1.0)
            v.memset(ones_c[:], 1.0)
            v.memset(ones_rb[:], 1.0)
            dv.inc(v.memset(ones_r[:], 1.0))
        block.vector(_consts)
        M_CONST = dv.n

        x_bf = xbuf
        with nc.sbuf_tensor("wq_q", [P, 2, C, SB], BF16) as wq_q, \
             nc.sbuf_tensor("wk_s", [P, C, HD], BF16) as wk_s, \
             nc.sbuf_tensor("wv_s", [P, C, HD], BF16) as wv_s, \
             nc.sbuf_tensor("sq_scr", [P, 2, S], BF16) as sq_scr, \
             nc.sbuf_tensor("m_row", [1, S], F32) as m_row, \
             nc.sbuf_tensor("r_row", [1, S], F32) as r_row, \
             nc.sbuf_tensor("m_rowb", [1, S], BF16) as m_rowb, \
             nc.sbuf_tensor("r_rowb", [1, S], BF16) as r_rowb, \
             nc.sbuf_tensor("m_bc", [P, S], BF16) as m_bc, \
             nc.sbuf_tensor("r_bc", [P, S], BF16) as r_bc:

            xstg = Q_sb[:].bitcast(F32).rearrange("p (a c) b -> p a (c b)", a=4)
            SCC0 = sc.n
            STAGED = [5, 6, 7, 8, 9, 2, 3, 10, 11, 12, 13, 14, 15]
            SC_CASTS = STAGED[0::2]
            DV_CASTS = STAGED[1::2]

            def stg_slot(c):
                return STAGED.index(c) % 4

            def stg_wait(c):
                i = STAGED.index(c)
                return s_xc[i % 4], 16 * (i // 4 + 1)

            xcast_sc = {c: SCC0 + i + 1 for i, c in enumerate(SC_CASTS)}
            M_MROW = SCC0 + len(SC_CASTS) + 1
            M_MROWB = M_MROW + 1
            M_RSTD1 = M_MROWB + 1
            M_BCEVM = M_RSTD1 + 1
            M_BCEVR0, M_BCEVR = M_RSTD1 + 2, M_RSTD1 + 3
            EVAC_ORDER = ["kA", "v0", "q0", "v1", "q2", "kB", "q1", "q3", "v2",
                          "q4", "q5", "v3", "q6", "q7", "q8", "q9", "q10", "q11"]
            sc_ev = {n: M_BCEVR + 1 + i for i, n in enumerate(EVAC_ORDER)}
            SC_P1_END = M_BCEVR + len(EVAC_ORDER)

            STAT_ORDER = [5, 6, 7, 0, 1, 8, 9, 2, 4, 3, 10, 11, 12, 13, 14, 15]
            DV0 = dv.n
            xcast_dv = {}
            sqmark = {}
            _n = DV0
            for _ci, _c in enumerate(STAT_ORDER):
                if _c in DV_CASTS:
                    _n += 1; xcast_dv[_c] = _n
                _n += 1; sqmark[_ci] = _n
            M_VARROW = _n + 1

            def _sy_xc(syn):
                for i, c in enumerate(STAGED):
                    if i >= 4:
                        pc = STAGED[i - 4]
                        if pc in SC_CASTS:
                            syn.wait_ge(s_sc, xcast_sc[pc])
                        else:
                            syn.wait_ge(s_dv, xcast_dv[pc])
                    syn.dma_start(out=xstg[:, i % 4, :],
                                  in_=xT_r[:, c, :]).then_inc(s_xc[i % 4], 16)
            block.sync(_sy_xc)

            def _sc_xc(s):
                s.wait_ge(s_dv, M_CONST)
                for c in SC_CASTS:
                    sm, val = stg_wait(c)
                    s.wait_ge(sm, val)
                    sc.inc(s.copy(x_bf[:, c, :], xstg[:, stg_slot(c), :]))
            block.scalar(_sc_xc)
            assert sc.n == xcast_sc[15]

            with nc.psum_tensor("st_ps", [P, S], F32) as st_ps, \
                 nc.psum_tensor("sq_ps", [P, S], F32) as sq_ps:

                assert DV0 == dv.n
                m2_scr = xstg[0:1, 0, :]

                def _te_p0(t):
                    t.wait_ge(s_dv, M_CONST)
                    for ci, c in enumerate(STAT_ORDER):
                        if c == 0:
                            t.wait_ge(s_x[0], 16)
                        elif c == 4:
                            t.wait_ge(s_x[2], 16)
                        elif c in SC_CASTS:
                            t.wait_ge(s_sc, xcast_sc[c])
                        elif c in DV_CASTS:
                            t.wait_ge(s_dv, xcast_dv[c])
                        for tb in range(4):
                            t.matmul(st_ps[0:1, tb * SB:(tb + 1) * SB], ones_c[:],
                                     x_bf[:, c, tb * SB:(tb + 1) * SB],
                                     start=(ci == 0), stop=(ci == C - 1))
                        t.wait_ge(s_dv, sqmark[ci])
                        ii = None
                        for tb in range(4):
                            ii = t.matmul(sq_ps[0:1, tb * SB:(tb + 1) * SB], ones_c[:],
                                          sq_scr[:, ci % 2, tb * SB:(tb + 1) * SB],
                                          start=(ci == 0), stop=(ci == C - 1))
                        marks["stat"][ci] = te.inc(ii)
                block.tensor(_te_p0)

                def _dv_p0a(v):
                    for ci, c in enumerate(STAT_ORDER):
                        if c in DV_CASTS:
                            sm, val = stg_wait(c)
                            v.wait_ge(sm, val)
                            dv.inc(v.tensor_copy(x_bf[:, c, :],
                                                 xstg[:, stg_slot(c), :]))
                        if c in (0, 1):
                            v.wait_ge(s_x[0], 16)
                        elif c == 4:
                            v.wait_ge(s_x[2], 16)
                        elif c in SC_CASTS:
                            v.wait_ge(s_sc, xcast_sc[c])
                        if ci >= 2:
                            v.wait_ge(s_te, marks["stat"][ci - 2])
                        dv.inc(v.tensor_mul(sq_scr[:, ci % 2, :], x_bf[:, c, :], x_bf[:, c, :]))
                        assert dv.n == sqmark[ci]
                    v.wait_ge(s_sc, M_MROW)
                    v.tensor_mul(m2_scr, m_row[:], m_row[:])
                    dv.inc(v.scalar_tensor_tensor(r_row[:], sq_ps[0:1, :],
                                                  cst[0:1, 1:2], m2_scr,
                                                  ALU.mult, ALU.subtract))
                    assert dv.n == M_VARROW
                block.vector(_dv_p0a)

                def _sc_rows(s):
                    s.wait_ge(s_te, marks["stat"][C - 1])
                    sc.inc(s.activation(m_row[:], st_ps[0:1, :], AF.Copy,
                                        scale=cst[0:1, 1:2]))
                    sc.inc(s.copy(m_rowb[:], m_row[:]))
                    s.wait_ge(s_dv, M_VARROW)
                    s.activation(r_row[:], r_row[:], AF.Ln, bias=cst[0:1, 0:1])
                    sc.inc(s.activation(r_rowb[:], r_row[:], AF.Exp, scale=cst[0:1, 3:4]))
                block.scalar(_sc_rows)
                assert sc.n == M_RSTD1

            with nc.psum_tensor("bcm_ps", [P, S], F32) as bcm_ps, \
                 nc.psum_tensor("bcr_ps", [P, S], F32) as bcr_ps:

                def _te_p0b(t):
                    t.wait_ge(s_sc, M_MROWB)
                    ii = None
                    for tb in range(4):
                        ii = t.matmul(bcm_ps[:, tb * SB:(tb + 1) * SB], ones_rb[:],
                                      m_rowb[0:1, tb * SB:(tb + 1) * SB], start=True, stop=True)
                    marks["bcm"] = te.inc(ii)
                    t.wait_ge(s_sc, M_RSTD1)
                    for tb in range(4):
                        ii = t.matmul(bcr_ps[:, tb * SB:(tb + 1) * SB], ones_rb[:],
                                      r_rowb[0:1, tb * SB:(tb + 1) * SB], start=True, stop=True)
                    marks["bcr"] = te.inc(ii)
                block.tensor(_te_p0b)

                def _sc_p0b(s):
                    s.wait_ge(s_te, marks["bcm"])
                    sc.inc(s.copy(m_bc[:], bcm_ps[:]))
                    s.wait_ge(s_te, marks["bcr"])
                    sc.inc(s.copy(r_bc[:, 0:SO], bcr_ps[:, 0:SO]))
                    sc.inc(s.copy(r_bc[:, SO:S], bcr_ps[:, SO:S]))
                block.scalar(_sc_p0b)
                assert sc.n == M_BCEVR

                DVA = dv.n

                def aph0_mark(c):
                    return DVA + c + 1

                def aph1_mark(c):
                    return DVA + C + c + 1

                def _dv_p0c(v):
                    v.wait_ge(s_sc, M_BCEVM)
                    for c in range(4):
                        v.tensor_sub(x_bf[:, c, 0:SO], x_bf[:, c, 0:SO], m_bc[:, 0:SO])
                    v.wait_ge(s_sc, M_BCEVR0)
                    for c in range(C):
                        if c >= 4:
                            v.tensor_sub(x_bf[:, c, 0:SO], x_bf[:, c, 0:SO], m_bc[:, 0:SO])
                        dv.inc(v.tensor_mul(x_bf[:, c, 0:SO], x_bf[:, c, 0:SO], r_bc[:, 0:SO]))
                    v.wait_ge(s_sc, M_BCEVR)
                    for c in range(C):
                        v.tensor_sub(x_bf[:, c, SO:S], x_bf[:, c, SO:S], m_bc[:, SO:S])
                        dv.inc(v.tensor_mul(x_bf[:, c, SO:S], x_bf[:, c, SO:S], r_bc[:, SO:S]))
                block.vector(_dv_p0c)
                assert dv.n == aph1_mark(C - 1)

            with nc.psum_tensor("p1_ps", [P, 4, SB], F32) as p1_ps, \
                 nc.psum_tensor("v_ps", [P, NT, HD], F32) as v_ps:

                def _te_p1(t):
                    def k_half(hh):
                        if hh == 0:
                            t.wait_ge(s_wk, 16)
                        ii = None
                        for c in range(C):
                            t.wait_ge(s_dv, aph0_mark(c) if hh == 0 else aph1_mark(c))
                            for tb in range(2):
                                ii = t.matmul(p1_ps[:, 2 * hh + tb, :], wk_s[:, c, :],
                                              x_bf[:, c, hh * SO + tb * SB:
                                                   hh * SO + (tb + 1) * SB],
                                              start=(c == 0), stop=(c == C - 1))
                        marks["kA" if hh == 0 else "kB"] = te.inc(ii)

                    def v_group(g):
                        if g == 0:
                            t.wait_ge(s_wv, 16)
                        ii = None
                        for tt in range(4 * g, 4 * g + 4):
                            for c in range(C):
                                ii = t.matmul(v_ps[:, tt, :], x_bf[:, c, tt * P:(tt + 1) * P],
                                              wv_s[:, c, :], start=(c == 0), stop=(c == C - 1))
                        marks["v"][g] = te.inc(ii)

                    def q_tile(f):
                        q, fi = f // 4, f % 4
                        if f in (0, 4, 8):
                            t.wait_ge(*wgate(f"wq{q}"))
                        if f == 0:
                            t.wait_ge(s_sc, sc_ev["kA"])
                        elif f == 1:
                            t.wait_ge(s_sc, sc_ev["kB"])
                        else:
                            t.wait_ge(s_sc, sc_ev[f"q{f - 2}"])
                        ii = None
                        for c in range(C):
                            for sb in range(NSB):
                                ii = t.matmul(p1_ps[:, 2 * (f % 2) + sb, :],
                                              wq_q[:, q % 2, c, fi * P:(fi + 1) * P],
                                              x_bf[:, c, sb * SB:(sb + 1) * SB],
                                              start=(c == 0), stop=(c == C - 1))
                        marks["q"][f] = te.inc(ii)

                    k_half(0)
                    v_group(0)
                    q_tile(0)
                    v_group(1)
                    q_tile(2)
                    k_half(1)
                    q_tile(1)
                    q_tile(3)
                    v_group(2)
                    q_tile(4)
                    q_tile(5)
                    v_group(3)
                    for f in (6, 7, 8, 9, 10, 11):
                        q_tile(f)
                    marks["p1_done"] = te.n
                block.tensor(_te_p1)

                def _sc_p1(s):
                    for name in EVAC_ORDER:
                        if name in ("kA", "kB"):
                            hh = 0 if name == "kA" else 1
                            s.wait_ge(s_te, marks[name])
                            ii = None
                            for tb in range(2):
                                ii = s.copy(K_sb[:, (2 * hh + tb) * SB:
                                                 (2 * hh + tb + 1) * SB],
                                            p1_ps[:, 2 * hh + tb, :])
                            sc.inc(ii)
                        elif name[0] == "v":
                            g = int(name[1:])
                            s.wait_ge(s_te, marks["v"][g])
                            sc.inc(s.copy(V_sb[:, 4 * g:4 * g + 4, :],
                                          v_ps[:, 4 * g:4 * g + 4, :]))
                        else:
                            f = int(name[1:])
                            s.wait_ge(s_te, marks["q"][f])
                            ii = None
                            for sb in range(NSB):
                                ii = s.copy(Q_sb[:, f, sb * SB:(sb + 1) * SB],
                                            p1_ps[:, 2 * (f % 2) + sb, :])
                            sc.inc(ii)
                        assert sc.n == sc_ev[name]
                block.scalar(_sc_p1)
                assert sc.n == SC_P1_END

            def _gp_a(g):
                g.dma_start(out=x_bf[:, 0:2, :],
                            in_=xT_r[:, 0:2, :]).then_inc(s_x[0], 16)
                g.dma_start(out=x_bf[:, 4:5, :],
                            in_=xT_r[:, 4:5, :]).then_inc(s_x[2], 16)
                g.dma_start(out=wk_s[:], in_=wk_r[:]).then_inc(s_wk, 16)
                g.dma_start(out=wv_s[:], in_=wv_r[:]).then_inc(s_wv, 16)
                for q in range(3):
                    if q >= 2:
                        g.wait_ge(s_te, marks["q"][4 * (q - 2) + 3])
                    g.dma_start(out=wq_q[:, q % 2],
                                in_=wq_r[:, :, q * SB:(q + 1) * SB]).then_inc(wgate(f"wq{q}")[0], 16)
            block.gpsimd(_gp_a)

            if debug:
                def _sy_dbg1(syn):
                    syn.wait_ge(s_dv, M_VARROW)
                    syn.wait_ge(s_sc, M_RSTD1)
                    syn.dma_start(out=dbg["m"], in_=m_row[:]).then_inc(s_y, 16)
                    syn.dma_start(out=dbg["r"], in_=r_row[:]).then_inc(s_y, 16)
                    syn.wait_ge(s_sc, SC_P1_END)
                    syn.dma_start(out=dbg["k"], in_=K_sb[:]).then_inc(s_y, 16)
                    syn.dma_start(out=dbg["v"], in_=V_sb[:]).then_inc(s_y, 16)
                    syn.dma_start(out=dbg["q"], in_=Q_sb[:]).then_inc(s_y, 16)
                block.sync(_sy_dbg1)

        if True:
            x_res = xbuf[:].bitcast(F32)
            m2_row = None
            r2_row = None

            with nc.sbuf_tensor("o_sb", [P, H, SO], BF16) as O_sb, \
                 nc.sbuf_tensor("wo_q", [P, 2, C, 256], BF16) as wo_q:

                wq3v = wo_q[:].rearrange("p a c f -> p (a c f)").rearrange(
                    "p (c f) -> p c f", f=SB)

                def _gp_wq3(g):
                    g.wait_ge(s_te, marks["p1_done"])
                    g.dma_start(out=wq3v,
                                in_=wq_r[:, :, 3 * SB:4 * SB]).then_inc(s_wq3, 16)
                block.gpsimd(_gp_wq3)

                with nc.sbuf_tensor("e_sb", [P, 2, NT * SB], BF16) as E, \
                     nc.sbuf_tensor("tsc", [P, 2, NT * SB // 2], BF16) as TSC, \
                     nc.psum_tensor("qk_ps", [P, 4, SB], F32) as qk_ps, \
                     nc.psum_tensor("pv_ps", [P, 2, SB], F32) as pv_ps, \
                     nc.psum_tensor("d_ps", [P, SB], F32) as d_ps, \
                     nc.psum_tensor("wo_ps", [P, SB], F32) as wo_ps:

                    SC2 = sc.n
                    DV2 = dv.n

                    def xcast2_mark(c):
                        return SC_P2_END + c + 1

                    expb_mark = {}
                    ln_mark = {}
                    expd_mark = {}
                    _n = SC2
                    for _i in range(NITER):
                        for _j in range(2):
                            _n += 1; expb_mark[8 * _i + _j] = _n
                        if _i >= 2:
                            _n += 1; ln_mark[_i - 2] = _n
                        for _j in (2, 3, 4, 5):
                            _n += 1; expb_mark[8 * _i + _j] = _n
                        if _i >= 2:
                            _n += 1; expd_mark[_i - 2] = _n
                        for _j in (6, 7):
                            _n += 1; expb_mark[8 * _i + _j] = _n
                    _n += 1; ln_mark[30] = _n
                    _n += 1; expd_mark[30] = _n
                    _n += 1; ln_mark[31] = _n
                    _n += 1; expd_mark[31] = _n
                    SC_P2_END = _n

                    tree_mark = {}
                    dsp_mark = {}
                    onorm_mark = {}
                    woep_mark = {}
                    qev2_mark = {}
                    _n = DV2
                    for _i in range(NITER):
                        if 2 <= _i <= 16 and _i % 2 == 0:
                            _n += 1; qev2_mark[(_i - 2) // 2] = _n
                        if _i >= 19:
                            _n += 1; woep_mark[_i - 19] = _n
                        if _i >= 2:
                            _n += 1; dsp_mark[_i - 2] = _n
                        if _i >= 1:
                            _n += 1; tree_mark[_i - 1] = _n
                        if _i >= 2:
                            _n += 1; onorm_mark[_i - 2] = _n
                    sq2_mark = {}
                    _n += 1; woep_mark[13] = _n
                    _n += 1; tree_mark[31] = _n
                    _n += 1; dsp_mark[30] = _n
                    _n += 1; onorm_mark[30] = _n
                    _n += 1; woep_mark[14] = _n
                    _n += 1; dsp_mark[31] = _n
                    _n += 1; onorm_mark[31] = _n
                    for _k in range(15, 32):
                        _n += 1; woep_mark[_k] = _n
                        if _k >= 16:
                            _n += 1; sq2_mark[_k - 16] = _n
                    DV_P2_END = _n

                    def wo_gp_gate(k):
                        rnd, p = k // 16, (k % 16) // 2
                        return wgate(f"wo{p}" if rnd == 0 else f"wob{p}")

                    def wo_lhsT(k, h):
                        p = (k % 16) // 2
                        f = k % 16
                        return wo_q[:, p % 2, h, (f % 2) * P:(f % 2 + 1) * P]

                    def _te_wo_group(t, k, h0, h1):
                        sbw = k // 16
                        out_ap = wo_ps[:] if k <= 15 else qk_ps[:, k % 4, :]
                        ii = None
                        for h in range(h0, h1):
                            ii = t.matmul(out_ap, wo_lhsT(k, h),
                                          O_sb[:, h, sbw * SB:(sbw + 1) * SB],
                                          start=(h == 0), stop=(h == H - 1))
                        return ii

                    def _te_p2(t):
                        t.wait_ge(s_dv, DV2)
                        t.wait_ge(s_sc, SC_P1_END)
                        for i in range(NITER):
                            sb, h = i // 16, i % 16
                            pj = (i - 1) % 2
                            qsl = Q_sb[:, h, sb * SB:(sb + 1) * SB]

                            def qk_batch(j, i=i, qsl=qsl):
                                g = 8 * i + j
                                if g >= 2:
                                    t.wait_ge(s_sc, expb_mark[g - 2])
                                ii = None
                                for u in range(2):
                                    c = 2 * j + u
                                    ii = t.matmul(qk_ps[:, c % 4, :],
                                                  K_sb[:, c * P:(c + 1) * P], qsl,
                                                  start=True, stop=True)
                                marks["qk"][g] = te.inc(ii)

                            def pv_part(c0, c1, i=i, pj=pj):
                                ii = None
                                for c in range(c0, c1):
                                    ii = t.matmul(pv_ps[:, pj, :], V_sb[:, c, :],
                                                  E[:, pj, c * SB:(c + 1) * SB],
                                                  start=(c == 0), stop=(c == NT - 1))
                                if c1 == NT:
                                    marks["pv"][i - 1] = te.inc(ii)

                            if i >= 2:
                                t.wait_ge(s_dv, tree_mark[i - 2])
                                if i >= 3:
                                    t.wait_ge(s_sc, expd_mark[i - 3])
                                marks["dones"][i - 2] = te.inc(
                                    t.matmul(d_ps[0:1, :], ones_c[:],
                                             TSC[:, (i - 2) % 2, 0:SB], start=True, stop=True))
                            if h >= 12:
                                t.wait_ge(s_dv, qev2_mark[2 * (h - 12)])
                                t.wait_ge(s_dv, qev2_mark[2 * (h - 12) + sb])
                            qk_batch(0)
                            qk_batch(1)
                            if i >= 1:
                                t.wait_ge(s_sc, expb_mark[8 * (i - 1) + 7])
                                if i >= 3:
                                    t.wait_ge(s_dv, onorm_mark[i - 3])
                                pv_part(0, 4)
                            qk_batch(2)
                            if i >= 1:
                                pv_part(4, 8)
                            qk_batch(3)
                            if i >= 1:
                                pv_part(8, 12)
                            qk_batch(4)
                            if i >= 1:
                                pv_part(12, 16)
                            qk_batch(5)
                            if i >= 2:
                                t.wait_ge(s_dv, dsp_mark[i - 2])
                                t.matmul(d_ps[:], ones_rb[:], drow_hi[:],
                                         start=True, stop=False)
                                marks["dbc"][i - 2] = te.inc(
                                    t.matmul(d_ps[:], ones_rb[:], drow_lo[:],
                                             start=False, stop=True))
                            qk_batch(6)
                            if i >= 18:
                                k = i - 18
                                if k == 0:
                                    t.wait_ge(s_dv, onorm_mark[15])
                                else:
                                    t.wait_ge(s_dv, woep_mark[k - 1])
                                if k % 2 == 0:
                                    t.wait_ge(*wo_gp_gate(k))
                                _te_wo_group(t, k, 0, 8)
                            qk_batch(7)
                            if i >= 18:
                                marks["wo"][i - 18] = te.inc(_te_wo_group(t, i - 18, 8, H))
                            if i <= 15:
                                g2, hh = i // 2, i % 2
                                f2, sb2 = 12 + g2 // 2, g2 % 2
                                if i == 0:
                                    t.wait_ge(s_wq3, 16)
                                if hh == 0 and g2 >= 1:
                                    t.wait_ge(s_dv, qev2_mark[g2 - 1])
                                ii = None
                                for c in range(8 * hh, 8 * hh + 8):
                                    ii = t.matmul(wo_ps[:],
                                                  wq3v[:, c, (f2 - 12) * P:(f2 - 11) * P],
                                                  x_bf[:, c, sb2 * SB:(sb2 + 1) * SB],
                                                  start=(c == 0), stop=(c == C - 1))
                                if hh == 1:
                                    marks["q2"][g2] = te.inc(ii)
                        t.wait_ge(s_sc, expb_mark[8 * NITER - 1])
                        t.wait_ge(s_dv, onorm_mark[NITER - 3])
                        pj = (NITER - 1) % 2
                        ii = None
                        for c in range(NT):
                            ii = t.matmul(pv_ps[:, pj, :], V_sb[:, c, :],
                                          E[:, pj, c * SB:(c + 1) * SB],
                                          start=(c == 0), stop=(c == NT - 1))
                        marks["pv"][NITER - 1] = te.inc(ii)

                        def tail_d(j2):
                            t.wait_ge(s_dv, tree_mark[j2])
                            t.wait_ge(s_sc, expd_mark[j2 - 1])
                            marks["dones"][j2] = te.inc(
                                t.matmul(d_ps[0:1, :], ones_c[:], TSC[:, j2 % 2, 0:SB],
                                         start=True, stop=True))

                        def tail_dbc(j2):
                            t.wait_ge(s_dv, dsp_mark[j2])
                            t.matmul(d_ps[:], ones_rb[:], drow_hi[:],
                                     start=True, stop=False)
                            marks["dbc"][j2] = te.inc(
                                t.matmul(d_ps[:], ones_rb[:], drow_lo[:],
                                         start=False, stop=True))

                        def tail_wo(k):
                            if k == 14:
                                t.wait_ge(s_dv, woep_mark[13])
                            elif k == 15:
                                t.wait_ge(s_dv, woep_mark[14])
                            else:
                                if k == 16:
                                    t.wait_ge(s_dv, onorm_mark[NITER - 1])
                                if k >= 20:
                                    t.wait_ge(s_dv, woep_mark[k - 4])
                            if k % 2 == 0:
                                t.wait_ge(*wo_gp_gate(k))
                            marks["wo"][k] = te.inc(_te_wo_group(t, k, 0, H))

                        def stats2_c(c):
                            t.wait_ge(s_dv, sq2_mark[c])
                            for tb in range(2):
                                t.matmul(pv_ps[0:1, tb, :], ones_c[:],
                                         xb2[:, c, tb * SB:(tb + 1) * SB],
                                         start=(c == 0), stop=(c == C - 1))
                            ii = None
                            for tb in range(2):
                                dst = d_ps[0:1, :] if tb == 0 else wo_ps[0:1, :]
                                ii = t.matmul(dst, ones_c[:],
                                              v_flat[:, (c % 2) * SO + tb * SB:
                                                     (c % 2) * SO + (tb + 1) * SB],
                                              start=(c == 0), stop=(c == C - 1))
                            marks["stat2"][c] = te.inc(ii)

                        tail_d(30)
                        tail_wo(14)
                        tail_dbc(30)
                        tail_d(31)
                        tail_wo(15)
                        tail_dbc(31)
                        for k in range(16, 32):
                            tail_wo(k)
                            if k == 18:
                                t.wait_ge(s_dv, onorm_mark[NITER - 1])
                                t.wait_ge(s_sc, expd_mark[NITER - 1])
                            if k >= 18:
                                stats2_c(k - 18)
                        stats2_c(14)
                        stats2_c(15)
                    block.tensor(_te_p2)

                    def _sy_xres(syn):
                        syn.wait_ge(s_te, marks["q2"][7])
                        syn.dma_start(out=x_res[:],
                                      in_=xT_r[:, :, 0:SO]).then_inc(s_xres, 16)
                    block.sync(_sy_xres)

                    def _sc_p2(s):
                        def d_ln(jj):
                            s.wait_ge(s_te, marks["dones"][jj])
                            if jj >= 1:
                                s.wait_ge(s_dv, dsp_mark[jj - 1])
                            sc.inc(s.activation(drow[0:1, :], d_ps[0:1, :], AF.Ln))
                            assert sc.n == ln_mark[jj]

                        def d_exp(jj):
                            s.wait_ge(s_te, marks["dbc"][jj])
                            if jj >= 1:
                                s.wait_ge(s_dv, onorm_mark[jj - 1])
                            sc.inc(s.activation(dbc_sb[:], d_ps[:], AF.Exp,
                                                scale=cst[:, 4:5]))
                            assert sc.n == expd_mark[jj]

                        for i in range(NITER):
                            pi = i % 2
                            if i >= 2:
                                s.wait_ge(s_te, marks["pv"][i - 2])
                                s.wait_ge(s_dv, tree_mark[i - 2])

                            def batch(j, i=i, pi=pi):
                                g = 8 * i + j
                                s.wait_ge(s_te, marks["qk"][g])
                                r = j % 2
                                sc.inc(s.activation(E[:, pi, (2 * j) * SB:(2 * j + 2) * SB],
                                                    qk_ps[:, 2 * r:2 * r + 2, :],
                                                    AF.Exp, scale=cst[:, 2:3]))
                                assert sc.n == expb_mark[g]

                            batch(0)
                            batch(1)
                            if i >= 2:
                                d_ln(i - 2)
                            for j in (2, 3, 4, 5):
                                batch(j)
                            if i >= 2:
                                d_exp(i - 2)
                            batch(6)
                            batch(7)
                        d_ln(30)
                        d_exp(30)
                        d_ln(31)
                        d_exp(31)
                        assert sc.n == SC_P2_END
                    block.scalar(_sc_p2)

                    def _po_p2(g):
                        for p in range(2):
                            g.wait_ge(s_te, marks["q2"][7])
                            g.dma_start(out=wo_q[:, p % 2],
                                        in_=wo_r[:, :, p * 256:(p + 1) * 256]).then_inc(
                                            wgate(f"wo{p}")[0], 16)
                        for p in range(2, 8):
                            g.wait_ge(s_te, marks["wo"][2 * (p - 2) + 1])
                            g.dma_start(out=wo_q[:, p % 2],
                                        in_=wo_r[:, :, p * 256:(p + 1) * 256]).then_inc(
                                            wgate(f"wo{p}")[0], 16)
                    block.gpsimd(_po_p2)

                    def _dv_p2(v):
                        half = NT * SB // 2

                        def wo_ep(k):
                            sbw, f = k // 16, k % 16
                            if k == 0:
                                v.wait_ge(s_xres, 16)
                            v.wait_ge(s_te, marks["wo"][k])
                            src_ap = wo_ps[:] if k <= 15 else qk_ps[:, k % 4, :]
                            dv.inc(v.tensor_add(x_res[:, f, sbw * SB:(sbw + 1) * SB],
                                                x_res[:, f, sbw * SB:(sbw + 1) * SB],
                                                src_ap))
                            assert dv.n == woep_mark[k]

                        def tree(jj):
                            pk = jj % 2
                            v.wait_ge(s_sc, expb_mark[8 * jj + 7])
                            if jj >= 2:
                                v.wait_ge(s_te, marks["dones"][jj - 2])
                            v.tensor_add(TSC[:, pk, :], E[:, pk, 0:half],
                                         E[:, pk, half:2 * half])
                            v.tensor_add(TSC[:, pk, 0:half // 2], TSC[:, pk, 0:half // 2],
                                         TSC[:, pk, half // 2:half])
                            v.tensor_add(TSC[:, pk, 0:half // 4], TSC[:, pk, 0:half // 4],
                                         TSC[:, pk, half // 4:half // 2])
                            dv.inc(v.tensor_add(TSC[:, pk, 0:half // 8], TSC[:, pk, 0:half // 8],
                                                TSC[:, pk, half // 8:half // 4]))
                            assert dv.n == tree_mark[jj]

                        def dsplit(jj):
                            v.wait_ge(s_sc, ln_mark[jj])
                            v.tensor_copy(drow_hi[:], drow[0:1, :])
                            dv.inc(v.tensor_sub(drow_lo[:], drow[0:1, :], drow_hi[:]))
                            assert dv.n == dsp_mark[jj]

                        def onorm(jj):
                            pk = jj % 2
                            sbp, hp = jj // 16, jj % 16
                            v.wait_ge(s_sc, expd_mark[jj])
                            dv.inc(v.tensor_mul(O_sb[:, hp, sbp * SB:(sbp + 1) * SB],
                                                pv_ps[:, pk, :], dbc_sb[:]))
                            assert dv.n == onorm_mark[jj]

                        for i in range(NITER):
                            if 2 <= i <= 16 and i % 2 == 0:
                                g2 = (i - 2) // 2
                                f2, sb2 = 12 + g2 // 2, g2 % 2
                                v.wait_ge(s_te, marks["q2"][g2])
                                dv.inc(v.tensor_copy(
                                    Q_sb[:, f2, sb2 * SB:(sb2 + 1) * SB], wo_ps[:]))
                                assert dv.n == qev2_mark[g2]
                            if i >= 19:
                                wo_ep(i - 19)
                            if i >= 2:
                                dsplit(i - 2)
                            if i >= 1:
                                tree(i - 1)
                            if i >= 2:
                                onorm(i - 2)
                        def sq2(c):
                            v.wait_ge(s_sc, xcast2_mark(c))
                            if c >= 2:
                                v.wait_ge(s_te, marks["stat2"][c - 2])
                            dv.inc(v.tensor_mul(v_flat[:, (c % 2) * SO:(c % 2 + 1) * SO],
                                                xb2[:, c, :], xb2[:, c, :]))
                            assert dv.n == sq2_mark[c]

                        wo_ep(13)
                        tree(31)
                        dsplit(30)
                        onorm(30)
                        wo_ep(14)
                        dsplit(31)
                        onorm(31)
                        for k in range(15, 32):
                            wo_ep(k)
                            if k >= 16:
                                sq2(k - 16)
                        assert dv.n == DV_P2_END
                    block.vector(_dv_p2)

                    def _gp_b(g):
                        for p in range(8):
                            if p < 2:
                                g.wait_ge(s_te, marks["wo"][13 if p == 0 else 15])
                            else:
                                g.wait_ge(s_te, marks["wo"][16 + 2 * (p - 2) + 1])
                            g.dma_start(out=wo_q[:, p % 2],
                                        in_=wo_r[:, :, p * 256:(p + 1) * 256]).then_inc(
                                            wgate(f"wob{p}")[0], 16)
                    block.gpsimd(_gp_b)

                    def _sc_p4cast(s):
                        for c in range(C):
                            s.wait_ge(s_dv, woep_mark[16 + c])
                            sc.inc(s.copy(xb2[:, c, :], x_res[:, c, :]))
                    block.scalar(_sc_p4cast)
                    assert sc.n == xcast2_mark(C - 1)

                    pv01 = pv_ps[0:1, :, :].rearrange("p a b -> p (a b)")
                    _tscf = TSC[:].bitcast(F32).rearrange("p a b -> p (a b)")
                    _tscb = TSC[:].rearrange("p a b -> p (a b)")
                    m2_row = _tscf[0:1, 0:SO]
                    r2_row = _tscf[0:1, SO:2 * SO]
                    m2_rowb = _tscb[0:1, 4 * SO:5 * SO]
                    r2_rowb = _tscb[0:1, 5 * SO:6 * SO]

                    m2sq_row = _tscf[0:1, 3 * SO:4 * SO]

                    def _dv_m2(v):
                        v.wait_ge(s_te, marks["stat2"][C - 1])
                        v.tensor_scalar_mul(m2_row, pv01, cst[0:1, 1:2])
                        v.tensor_copy(m2_rowb, m2_row)
                        v.tensor_mul(m2sq_row, m2_row, m2_row)
                        v.scalar_tensor_tensor(_tscf[0:1, SO:SO + SB], d_ps[0:1, :],
                                               cst[0:1, 1:2],
                                               _tscf[0:1, 3 * SO:3 * SO + SB],
                                               ALU.mult, ALU.subtract)
                        dv.inc(v.scalar_tensor_tensor(
                            _tscf[0:1, SO + SB:2 * SO], wo_ps[0:1, :],
                            cst[0:1, 1:2], _tscf[0:1, 3 * SO + SB:4 * SO],
                            ALU.mult, ALU.subtract))
                    block.vector(_dv_m2)
                    M_VARROW2 = dv.n

                    def _sc_m2(s):
                        s.wait_ge(s_dv, M_VARROW2)
                        s.activation(r2_row, r2_row, AF.Ln, bias=cst[0:1, 0:1])
                        sc.inc(s.activation(r2_rowb, r2_row, AF.Exp,
                                            scale=cst[0:1, 3:4]))
                    block.scalar(_sc_m2)
                    M_RSTD2 = sc.n

                if debug:
                    def _sy_dbg2(syn):
                        syn.wait_ge(s_dv, onorm_mark[NITER - 1])
                        syn.dma_start(out=dbg["o"], in_=O_sb[:]).then_inc(s_y, 16)
                        syn.wait_ge(s_dv, DV_P2_END)
                        syn.dma_start(out=dbg["x1"], in_=x_res[:]).then_inc(s_y, 16)
                    block.sync(_sy_dbg2)

            with nc.sbuf_tensor("wt_q", [P, 2, C, 256], BF16) as wt_q, \
                 nc.sbuf_tensor("m2_bc", [P, SO], BF16) as m2_bc, \
                 nc.sbuf_tensor("r2_bc", [P, SO], BF16) as r2_bc, \
                 nc.sbuf_tensor("sw_row", [1, D], F32) as sw_row, \
                 nc.sbuf_tensor("bias_row", [1, D], F32) as bias_row, \
                 nc.sbuf_tensor("sw_col", [P, C], F32) as sw_col, \
                 nc.sbuf_tensor("bias_col", [P, C], F32) as bias_col, \
                 nc.sbuf_tensor("t1_scr", [P, 2, SB], F32) as t1_scr:

                DV4 = dv.n

                with nc.psum_tensor("bc2m_ps", [P, SO], F32) as bc2m_ps, \
                     nc.psum_tensor("bc2r_ps", [P, SO], F32) as bc2r_ps, \
                     nc.psum_tensor("swc_ps", [P, C], F32) as swc_ps, \
                     nc.psum_tensor("bic_ps", [P, C], F32) as bic_ps:

                    def _te_p4b(t):
                        t.wait_ge(s_sc, M_RSTD2)
                        ii = None
                        for tb in range(2):
                            t.matmul(bc2m_ps[:, tb * SB:(tb + 1) * SB], ones_rb[:],
                                     m2_rowb[:, tb * SB:(tb + 1) * SB], start=True, stop=True)
                            ii = t.matmul(bc2r_ps[:, tb * SB:(tb + 1) * SB], ones_rb[:],
                                          r2_rowb[:, tb * SB:(tb + 1) * SB], start=True, stop=True)
                        marks["bc2"] = te.inc(ii)
                        t.wait_ge(s_swr, 16)
                        t.wait_ge(s_bir, 16)
                        for c in range(C):
                            t.matmul(swc_ps[:, c:c + 1], sw_row[0:1, c * P:(c + 1) * P],
                                     ones_r[0:1, 0:1], start=(c == 0), stop=(c == C - 1))
                        ii = None
                        for c in range(C):
                            ii = t.matmul(bic_ps[:, c:c + 1], bias_row[0:1, c * P:(c + 1) * P],
                                          ones_r[0:1, 0:1], start=(c == 0), stop=(c == C - 1))
                        marks["cols"] = te.inc(ii)
                    block.tensor(_te_p4b)

                    def _sc_p4ev(s):
                        s.wait_ge(s_te, marks["bc2"])
                        sc.inc(s.copy(m2_bc[:], bc2m_ps[:]))
                        sc.inc(s.copy(r2_bc[:], bc2r_ps[:]))
                    block.scalar(_sc_p4ev)
                    M_BC2EV = sc.n

                    DVB = dv.n

                    def apply2_mark(c):
                        return DVB + 1 + c + 1

                    def _dv_p4c(v):
                        v.wait_ge(s_te, marks["cols"])
                        v.tensor_copy(sw_col[:], swc_ps[:])
                        dv.inc(v.tensor_copy(bias_col[:], bic_ps[:]))
                        v.wait_ge(s_sc, M_BC2EV)
                        for c in range(C):
                            v.tensor_sub(xb2[:, c, :], xb2[:, c, :], m2_bc[:])
                            dv.inc(v.tensor_mul(xb2[:, c, :], xb2[:, c, :], r2_bc[:]))
                    block.vector(_dv_p4c)
                    M_COLS = dv.n
                    assert M_COLS == apply2_mark(C - 1)

                with nc.psum_tensor("ffn_ps", [P, 4, SB], F32) as ffn_ps:
                    DVF = dv.n

                    def ffep_mark(k):
                        return DVF + k + 1

                    def _te_p4c(t):
                        for g in range(8):
                            for k in range(4 * g, 4 * g + 4):
                                o, sb = k // 2, k % 2
                                if k % 4 == 0:
                                    t.wait_ge(*wgate(f"wt{g}"))
                                if k == 0:
                                    t.wait_ge(s_dv, apply2_mark(7))
                                if k >= 4:
                                    t.wait_ge(s_dv, ffep_mark(k - 4))
                                for c in range(8):
                                    t.matmul(ffn_ps[:, k % 4, :],
                                             wt_q[:, g % 2, c, (o % 2) * P:(o % 2 + 1) * P],
                                             xb2[:, c, sb * SB:(sb + 1) * SB],
                                             start=(c == 0), stop=False)
                            for k in range(4 * g, 4 * g + 4):
                                o, sb = k // 2, k % 2
                                if k == 0:
                                    t.wait_ge(s_dv, M_COLS)
                                ii = None
                                for c in range(8, C):
                                    ii = t.matmul(ffn_ps[:, k % 4, :],
                                                  wt_q[:, g % 2, c, (o % 2) * P:(o % 2 + 1) * P],
                                                  xb2[:, c, sb * SB:(sb + 1) * SB],
                                                  start=False, stop=(c == C - 1))
                                marks["ffn"][k] = te.inc(ii)
                    block.tensor(_te_p4c)

                    def _dv_p4d(v):
                        for k in range(32):
                            o, sb = k // 2, k % 2
                            v.wait_ge(s_te, marks["ffn"][k])
                            v.tensor_scalar(t1_scr[:, k % 2, :], ffn_ps[:, k % 4, :],
                                            sw_col[:, o:o + 1], bias_col[:, o:o + 1],
                                            ALU.mult, ALU.add)
                            dv.inc(v.tensor_add(x_res[:, o, sb * SB:(sb + 1) * SB],
                                                x_res[:, o, sb * SB:(sb + 1) * SB],
                                                t1_scr[:, k % 2, :]))
                    block.vector(_dv_p4d)
                    M_FFN_DONE = dv.n
                    assert M_FFN_DONE == ffep_mark(31)

                def _gp_c(g):
                    for q in range(8):
                        if q < 2:
                            g.wait_ge(s_te, marks["wo"][31])
                        else:
                            g.wait_ge(s_te, marks["ffn"][4 * (q - 2) + 3])
                        g.dma_start(out=wt_q[:, q % 2],
                                    in_=wt_r[:, :, q * 256:(q + 1) * 256]).then_inc(
                                        wgate(f"wt{q}")[0], 16)
                    for o in range(1, C, 2):
                        g.wait_ge(s_dv, ffep_mark(2 * o + 1))
                        g.dma_start(out=y_r[:, o:o + 1, :],
                                    in_=x_res[:, o:o + 1, :]).then_inc(s_y, 16)
                block.gpsimd(_gp_c)

                def _sy_b(syn):
                    syn.wait_ge(s_te, marks["wo"][31])
                    syn.dma_start(out=sw_row[:], in_=sw[:]).then_inc(s_swr, 16)
                    syn.dma_start(out=bias_row[:], in_=bias[:]).then_inc(s_bir, 16)
                    ny = 7 * 16 if debug else 0
                    for o in range(0, C, 2):
                        syn.wait_ge(s_dv, ffep_mark(2 * o + 1))
                        syn.dma_start(out=y_r[:, o:o + 1, :],
                                      in_=x_res[:, o:o + 1, :]).then_inc(s_y, 16)
                        ny += 16
                    ny += 8 * 16
                    syn.wait_ge(s_y, ny)
                block.sync(_sy_b)

    return nc


_NC_CACHE = {}


def kernel(L0_attn_Q_input, L0_attn_Wq, L0_attn_Wk, L0_attn_Wv, L0_attn_Wo,
           L0_ffn_weight, L0_ffn_scale_weight, L0_ffn_bias):
    x = np.asarray(L0_attn_Q_input, dtype=np.float32)
    wq_h = np.ascontiguousarray(np.asarray(L0_attn_Wq, dtype=np.float32))
    wk_h = np.ascontiguousarray(np.asarray(L0_attn_Wk, dtype=np.float32))
    wv_h = np.ascontiguousarray(np.asarray(L0_attn_Wv, dtype=np.float32))
    wo_h = np.ascontiguousarray(np.asarray(L0_attn_Wo, dtype=np.float32))
    wt_h = np.ascontiguousarray(np.asarray(L0_ffn_weight, dtype=np.float32).T)
    sw_h = np.ascontiguousarray(np.asarray(L0_ffn_scale_weight, dtype=np.float32).reshape(1, D))
    bi_h = np.ascontiguousarray(np.asarray(L0_ffn_bias, dtype=np.float32).reshape(1, D))

    if "nc" not in _NC_CACHE:
        _NC_CACHE["nc"] = build_nc(debug=False)
    nc = _NC_CACHE["nc"]

    in_maps = []
    for core in range(8):
        b, half = core // 2, core % 2
        xTb = x[b].T
        if half == 0:
            xp = np.ascontiguousarray(xTb)
        else:
            xp = np.ascontiguousarray(np.concatenate([xTb[:, SO:], xTb[:, :SO]], axis=1))
        in_maps.append({"xT": xp, "wq": wq_h, "wk": wk_h, "wv": wv_h,
                        "wo": wo_h, "wt": wt_h, "sw": sw_h, "bias": bi_h})

    res = run_bass_kernel_spmd(nc, in_maps, core_ids=list(range(8)))

    out = np.empty((4, S, D), dtype=np.float32)
    for core in range(8):
        b, half = core // 2, core % 2
        shard = np.asarray(res.results[core]["y"])
        out[b, half * SO:(half + 1) * SO, :] = shard.T
    return out
